# version 11
# baseline (speedup 1.0000x reference)
import math
import numpy as np
import ml_dtypes
from contextlib import ExitStack

import concourse.bass as bass
import concourse.mybir as mybir
from concourse.bass_utils import run_bass_kernel_spmd

F32 = mybir.dt.float32
BF16 = mybir.dt.bfloat16
I32 = mybir.dt.int32
AF = mybir.ActivationFunctionType
ALU = mybir.AluOpType
AXL = mybir.AxisListType

P = 128
S = 4096
D = 2048
NH = 16
HD = 128
IFF = 5504
NOC = IFF // P
NST = S // P
NDT = D // P
INV_SQRT_HD = 1.0 / math.sqrt(HD)
EXP_SHIFT = -25.0
LN_EPS = 1e-5

ENGS = ("sync", "tensor", "vector", "scalar", "gpsimd")
OGROUPS = [(g * 512, min(512, IFF - g * 512)) for g in range((IFF + 511) // 512)]
SEM_NAMES = ("bar", "pe", "dve", "act", "pool",
             "la", "lb", "lc", "ld4", "sa", "sb", "wa", "wb", "wc", "wd", "ga", "gb", "gc",
             "ca", "cb", "cc", "na", "nb")


def build(upto="full"):
    nc = bass.Bass("TRN2", target_bir_lowering=False)

    x_in = nc.dram_tensor("x", [S, D], F32, kind="ExternalInput").ap()
    gate_sh = nc.dram_tensor("gate_sh", [768, D], I32, kind="ExternalInput").ap()
    up_sh = nc.dram_tensor("up_sh", [768, D], I32, kind="ExternalInput").ap()
    down_sh = nc.dram_tensor("down_sh", [256, IFF], I32, kind="ExternalInput").ap()
    lut_in = nc.dram_tensor("lut", [1, 16], F32, kind="ExternalInput").ap()
    wcod = {}
    wscl = {}
    for nm in ("wq", "wk", "wv", "wo"):
        wcod[nm] = nc.dram_tensor(nm + "_codes", [D, D], F32, kind="ExternalInput").ap()
        wscl[nm] = nc.dram_tensor(nm + "_scales", [64, D], F32, kind="ExternalInput").ap()
    bq_in = nc.dram_tensor("bq", [1, D], F32, kind="ExternalInput").ap()
    bk_in = nc.dram_tensor("bk", [1, D], F32, kind="ExternalInput").ap()
    bv_in = nc.dram_tensor("bv", [1, D], F32, kind="ExternalInput").ap()
    bo_in = nc.dram_tensor("bo", [1, D], F32, kind="ExternalInput").ap()
    out = nc.dram_tensor("out", [S, D], F32, kind="ExternalOutput").ap()

    wg_shd = nc.dram_tensor("wg_shd", [768, D], BF16).ap()
    wu_shd = nc.dram_tensor("wu_shd", [768, D], BF16).ap()
    wd_shd = nc.dram_tensor("wd_shd", [256, IFF], BF16).ap()
    wg_nat = nc.dram_tensor("wg_nat", [6144, D], BF16, addr_space="Shared").ap()
    wu_nat = nc.dram_tensor("wu_nat", [6144, D], BF16, addr_space="Shared").ap()
    wd_nat = nc.dram_tensor("wd_nat", [D, IFF], BF16, addr_space="Shared").ap()
    wdT_nat = nc.dram_tensor("wdT_nat", [IFF, D], BF16).ap()
    hT_s = nc.dram_tensor("hT_s", [NOC, P, S], BF16).ap()
    x1_s = nc.dram_tensor("x1_s", [S, D], F32).ap()
    x2_s = nc.dram_tensor("x2_s", [S, D], F32).ap()
    qT_sc = nc.dram_tensor("qT_sc", [D, S], BF16).ap()
    rec_dr = [nc.dram_tensor(f"rec_dr{i}", [1, 512], F32).ap() for i in range(2)]
    kT_sc = nc.dram_tensor("kT_sc", [D, S], BF16).ap()
    v_sc = nc.dram_tensor("v_sc", [NH, S, HD], BF16).ap()

    es = ExitStack()

    ident = nc.alloc_sbuf_tensor("ident", [P, P], BF16).ap()
    ones_row = nc.alloc_sbuf_tensor("ones_row", [1, P], F32).ap()
    ones_col = nc.alloc_sbuf_tensor("ones_col", [P, 1], BF16).ap()
    one_one = nc.alloc_sbuf_tensor("one_one", [1, 1], F32).ap()
    lut_f = nc.alloc_sbuf_tensor("lut_f", [P, 16], F32).ap()
    mbias = nc.alloc_sbuf_tensor("mbias", [P, 1], F32).ap()
    eps_t = nc.alloc_sbuf_tensor("eps_t", [P, 1], F32).ap()
    bqT = nc.alloc_sbuf_tensor("bqT", [P, NDT], F32).ap()
    bkT = nc.alloc_sbuf_tensor("bkT", [P, NDT], F32).ap()
    bo_bc = nc.alloc_sbuf_tensor("bo_bc", [P, D], F32).ap()
    bv_bc = nc.alloc_sbuf_tensor("bv_bc", [P, D], F32).ap()

    sems = {k: es.enter_context(nc.semaphore(k)) for k in SEM_NAMES}
    C = {k: 0 for k in SEM_NAMES}
    NBLK = [0]

    def run_block(name, phase_fn):
        snap = dict(C)
        results = []
        with nc.Block(name) as blk:
            for ename in ENGS:
                C.clear()
                C.update(snap)

                def body(e, ename=ename):
                    if NBLK[0] > 0:
                        e.wait_ge(sems["bar"], 5 * NBLK[0])
                    phase_fn(ename, e)
                    e.nop().then_inc(sems["bar"], 1)

                getattr(blk, ename)(body)
                results.append(dict(C))
        for r in results[1:]:
            assert r == results[0], (name, r, results[0])
        C.clear()
        C.update(results[0])
        NBLK[0] += 1

    def INC(sem, bi, k=1):
        C[sem] += k
        if bi is not None:
            bi.then_inc(sems[sem], k)
        return C[sem]

    def W(e, sem, val):
        e.wait_ge(sems[sem], val)

    def EDGE(on, e, sem, bi):
        m = INC(sem, bi)
        if on:
            e.wait_ge(sems[sem], m)
        return m

    es0 = ExitStack()
    b_rows = es0.enter_context(nc.sbuf_tensor("b_rows", [1, 4 * D], F32)).ap()
    ps_init = es0.enter_context(nc.psum_tensor("ps_init", [P, 2 * NDT], F32)).ap()
    ps_bb = es0.enter_context(nc.psum_tensor("ps_bb", [P, D], F32)).ap()

    def p0(en, e):
        my = lambda n: en == n
        for dst, src in ((lut_f[:], lut_in.broadcast_to([P, 16])),
                         (b_rows[:, 0:D], bq_in), (b_rows[:, D:2 * D], bk_in),
                         (b_rows[:, 2 * D:3 * D], bv_in),
                         (b_rows[:, 3 * D:4 * D], bo_in)):
            INC("la", e.dma_start(out=dst, in_=src) if my("sync") else None, 16)
        ld_all = C["la"]
        bi = e.memset(ident[:], 0.0) if my("gpsimd") else None
        mz = INC("pool", bi)
        if my("gpsimd"):
            e.memset(ones_row[:], 1.0)
            e.memset(ones_col[:], 1.0)
            e.memset(one_one[:], 1.0)
            e.memset(mbias[:], EXP_SHIFT)
            e.memset(eps_t[:], LN_EPS)
            e.wait_ge(sems["pool"], mz)
            bi = e.affine_select(out=ident[:], in_=ident[:],
                                 compare_op=ALU.not_equal, fill=1.0, base=0,
                                 pattern=[[-1, P]], channel_multiplier=1)
        else:
            bi = None
        pool_done = INC("pool", bi)

        if my("tensor"):
            W(e, "la", ld_all)
            W(e, "pool", pool_done)
        for j, base in enumerate((0, D)):
            for fc in range(NDT):
                bi = (e.matmul(ps_init[:, j * NDT + fc:j * NDT + fc + 1],
                               b_rows[:, base + fc * P:base + (fc + 1) * P],
                               one_one[:], start=True, stop=True)
                      if my("tensor") else None)
        pe_bT = INC("pe", bi)
        for c in range(4):
            bi = (e.matmul(ps_bb[:, c * 512:(c + 1) * 512], ones_row[:],
                           b_rows[:, 3 * D + c * 512:3 * D + (c + 1) * 512],
                           start=True, stop=True) if my("tensor") else None)
        pe_bo = INC("pe", bi)
        if my("vector"):
            W(e, "pe", pe_bo)
            e.tensor_copy(out=bqT[:], in_=ps_init[:, 0:NDT])
            e.tensor_copy(out=bkT[:], in_=ps_init[:, NDT:2 * NDT])
            bi = e.tensor_copy(out=bo_bc[:], in_=ps_bb[:])
        else:
            bi = None
        dve_bo = INC("dve", bi)
        if my("tensor"):
            W(e, "dve", dve_bo)
        for c in range(4):
            bi = (e.matmul(ps_bb[:, c * 512:(c + 1) * 512], ones_row[:],
                           b_rows[:, 2 * D + c * 512:2 * D + (c + 1) * 512],
                           start=True, stop=True) if my("tensor") else None)
        pe_bv = INC("pe", bi)
        if my("vector"):
            W(e, "pe", pe_bv)
            bi = e.tensor_copy(out=bv_bc[:], in_=ps_bb[:])
        else:
            bi = None
        INC("dve", bi)

    run_block("p0", p0)
    es0.close()

    def make_ln_phase(x_src, xT_dst, bufs, ps_tr, deq=None):
        xin, sq_scr, xln, nsum_t, nmean_t, rstd_t, sc1, sumsq_t, var_t = bufs
        xT_v = xT_dst.rearrange("p (t s) -> p t s", t=NDT)
        if deq is not None:
            idx_b, idx_bf, tcon, acc_b = deq
            deq_units = ([(gate_sh[c * P:(c + 1) * P, :],
                           wg_shd[c * P:(c + 1) * P, :], D) for c in range(6)] +
                         [(up_sh[c * P:(c + 1) * P, :],
                           wu_shd[c * P:(c + 1) * P, :], D) for c in range(6)])
            deq_units_wd = [(down_sh[c * P:(c + 1) * P, c0:min(c0 + D, IFF)],
                             wd_shd[c * P:(c + 1) * P, c0:min(c0 + D, IFF)],
                             min(c0 + D, IFF) - c0)
                            for c in range(2) for c0 in range(0, IFF, D)]

        def ph(en, e):
            my = lambda n: en == n
            dve_marks, ld_marks = [], []
            dqi = [0]
            dq_st_marks = {}
            dq_cast_marks = {}

            def emit_deq(srcap, dstap, Wd_):
                i = dqi[0]
                par = i % 2
                lsem = "lc" if par == 0 else "ld4"
                ssem = "sa" if par == 0 else "sb"
                if my("sync"):
                    if i >= 2:
                        W(e, "dve", dq_cast_marks[i - 2])
                    bi = e.dma_start(out=idx_b[par][:, 0:Wd_], in_=srcap)
                else:
                    bi = None
                m_ld = INC(lsem, bi, 16)
                if my("vector"):
                    W(e, lsem, m_ld)
                    bi = e.tensor_copy(out=idx_bf[:, 0:Wd_],
                                       in_=idx_b[par][:, 0:Wd_])
                else:
                    bi = None
                dq_cast_marks[i] = EDGE(my("vector"), e, "dve", bi)
                if my("vector") and i >= 2:
                    W(e, dq_st_marks[i - 2][0], dq_st_marks[i - 2][1])
                bi = (e.tensor_scalar(out=acc_b[par][:, 0:Wd_],
                                      in0=idx_bf[:, 0:Wd_],
                                      scalar1=0.0, scalar2=lut_f[:, 0:1],
                                      op0=ALU.is_equal, op1=ALU.mult)
                      if my("vector") else None)
                EDGE(my("vector"), e, "dve", bi)
                for j in range(1, 16):
                    bi = (e.tensor_scalar(out=tcon[:, 0:Wd_],
                                          in0=idx_bf[:, 0:Wd_],
                                          scalar1=float(j),
                                          scalar2=lut_f[:, j:j + 1],
                                          op0=ALU.is_equal, op1=ALU.mult)
                          if my("vector") else None)
                    EDGE(my("vector"), e, "dve", bi)
                    bi = (e.tensor_tensor(out=acc_b[par][:, 0:Wd_],
                                          in0=acc_b[par][:, 0:Wd_],
                                          in1=tcon[:, 0:Wd_], op=ALU.add)
                          if my("vector") else None)
                    m_dve = EDGE(my("vector"), e, "dve", bi)
                if my("scalar"):
                    W(e, "dve", m_dve)
                    bi = e.dma_start(out=dstap, in_=acc_b[par][:, 0:Wd_])
                else:
                    bi = None
                dq_st_marks[i] = (ssem, INC(ssem, bi, 16))
                dqi[0] += 1
            for st in range(NST):
                par = st % 2
                lsem = "la" if par == 0 else "lb"
                dm, am, pm = [], [], []
                if my("sync"):
                    if st >= 2:
                        W(e, "dve", dve_marks[st - 2][4])
                    bi = e.dma_start(out=xin[par][:],
                                     in_=x_src[st * P:(st + 1) * P, :])
                else:
                    bi = None
                ld_marks.append(INC(lsem, bi, 16))
                if my("vector"):
                    W(e, lsem, ld_marks[st])
                    bi = e.reduce_sum(out=nsum_t[:, st:st + 1], in_=xin[par][:],
                                      axis=AXL.X, negate=True)
                else:
                    bi = None
                dm.append(INC("dve", bi))
                if my("scalar"):
                    W(e, lsem, ld_marks[st])
                    bi = e.activation(out=sq_scr[:], in_=xin[par][:],
                                      func=AF.Square,
                                      accum_out=sumsq_t[:, st:st + 1])
                else:
                    bi = None
                am.append(INC("act", bi))
                if my("vector"):
                    W(e, "dve", dm[0])
                    bi = e.tensor_scalar(out=nmean_t[:, st:st + 1],
                                         in0=nsum_t[:, st:st + 1],
                                         scalar1=1.0 / D, scalar2=None,
                                         op0=ALU.mult)
                else:
                    bi = None
                dm.append(INC("dve", bi))
                if my("vector"):
                    W(e, "dve", dm[1])
                    bi = e.tensor_scalar(out=sc1[:, 1:2],
                                         in0=nmean_t[:, st:st + 1],
                                         scalar1=nmean_t[:, st:st + 1],
                                         scalar2=None, op0=ALU.mult)
                else:
                    bi = None
                dm.append(INC("dve", bi))
                if my("vector"):
                    W(e, "dve", dm[2])
                    W(e, "act", am[0])
                    bi = e.scalar_tensor_tensor(out=var_t[:, st:st + 1],
                                                in0=sumsq_t[:, st:st + 1],
                                                scalar=1.0 / D, in1=sc1[:, 1:2],
                                                op0=ALU.mult, op1=ALU.subtract)
                else:
                    bi = None
                dm.append(INC("dve", bi))
                if my("scalar"):
                    W(e, "dve", dm[3])
                    bi = e.activation(out=sc1[:, 3:4], in_=var_t[:, st:st + 1],
                                      func=AF.Ln, bias=eps_t[:, 0:1])
                else:
                    bi = None
                am.append(INC("act", bi))
                if my("scalar"):
                    W(e, "act", am[1])
                    bi = e.activation(out=rstd_t[:, st:st + 1], in_=sc1[:, 3:4],
                                      func=AF.Exp, scale=-0.5)
                else:
                    bi = None
                am.append(INC("act", bi))
                if my("vector"):
                    W(e, "act", am[2])
                    bi = e.tensor_scalar(out=xln[par][:], in0=xin[par][:],
                                         scalar1=nmean_t[:, st:st + 1],
                                         scalar2=rstd_t[:, st:st + 1],
                                         op0=ALU.add, op1=ALU.mult)
                else:
                    bi = None
                dm.append(INC("dve", bi))
                for half in range(2):
                    if my("tensor"):
                        W(e, "dve", dm[4])
                        if st >= 1:
                            W(e, "dve", dve_marks[st - 1][5 + half])
                        for b in range(8):
                            ht = half * 8 + b
                            bi = e.matmul(
                                ps_tr[half][:, b * P:(b + 1) * P],
                                xln[par][:, ht * P:(ht + 1) * P], ident[:],
                                is_transpose=True, start=(b == 0), stop=(b == 7))
                    else:
                        bi = None
                    pm.append(INC("pe", bi))
                for half in range(2):
                    if my("vector"):
                        W(e, "pe", pm[half])
                        bi = e.tensor_copy(
                            out=xT_v[:, half * 8:(half + 1) * 8,
                                     st * P:(st + 1) * P],
                            in_=ps_tr[half].rearrange("p (b c) -> p b c", b=8))
                    else:
                        bi = None
                    dm.append(INC("dve", bi))
                dve_marks.append(dm)
                if deq is not None and st % 2 == 1 and st // 2 < len(deq_units):
                    u = deq_units[st // 2]
                    emit_deq(u[0], u[1], u[2])
            if deq is not None:
                for srcap, dstap, Wd_ in deq_units[NST // 2:]:
                    emit_deq(srcap, dstap, Wd_)
                gu_sa, gu_sb = C["sa"], C["sb"]
                for srcap, dstap, Wd_ in deq_units_wd:
                    emit_deq(srcap, dstap, Wd_)
                if my("gpsimd"):
                    W(e, "sa", gu_sa)
                    W(e, "sb", gu_sb)
                    bi = e.collective_compute(
                        "AllGather", ALU.bypass,
                        replica_groups=[list(range(8))],
                        ins=[wg_shd], outs=[wg_nat])
                else:
                    bi = None
                INC("ca", bi, 1)
                bi = (e.collective_compute(
                    "AllGather", ALU.bypass,
                    replica_groups=[list(range(8))],
                    ins=[wu_shd], outs=[wu_nat]) if my("gpsimd") else None)
                INC("cb", bi, 1)
                if my("gpsimd"):
                    W(e, "sa", C["sa"])
                    W(e, "sb", C["sb"])
                    bi = e.collective_compute(
                        "AllGather", ALU.bypass,
                        replica_groups=[list(range(8))],
                        ins=[wd_shd], outs=[wd_nat])
                else:
                    bi = None
                INC("cc", bi, 1)
        return ph

    def make_gateup_phase(xT_sb, wT_g, wT_u, silu_b, up_b, h_b, ps_g, ps_u,
                          wdtf_b=None):
        def ph(en, e):
            my = lambda n: en == n
            it = 0
            pe_marks, act_marks, pool_marks, st_marks, ldw_marks = [], [], [], [], []
            ntf_marks = {}
            for oc in range(NOC):
                wpar = oc % 2
                wlsem = "lc" if wpar == 0 else "ld4"
                if my("sync"):
                    if oc >= 2:
                        W(e, "pe", pe_marks[(oc - 2) * 8 + 7])
                    if wdtf_b is not None and oc == 0:
                        W(e, "ca", C["ca"])
                        W(e, "cb", C["cb"])
                for ht in range(NDT):
                    bi = (e.dma_start(
                        out=wT_g[wpar][:, ht * P:(ht + 1) * P],
                        in_=wg_nat[oc * P:(oc + 1) * P, ht * P:(ht + 1) * P],
                        transpose=True) if my("sync") else None)
                    INC(wlsem, bi, 16)
                    bi = (e.dma_start(
                        out=wT_u[wpar][:, ht * P:(ht + 1) * P],
                        in_=wu_nat[oc * P:(oc + 1) * P, ht * P:(ht + 1) * P],
                        transpose=True) if my("sync") else None)
                    INC(wlsem, bi, 16)
                ldw_marks.append(C[wlsem])
                if wdtf_b is not None:
                    npar = oc % 2
                    nsem = "na" if npar == 0 else "nb"
                    if my("sync"):
                        if oc == 0:
                            W(e, "cc", C["cc"])
                        if oc >= 2:
                            W(e, nsem, ntf_marks[oc - 2][1])
                        bi = e.dma_start(out=wdtf_b[npar][:],
                                         in_=wd_nat[:, oc * P:(oc + 1) * P],
                                         transpose=True)
                    else:
                        bi = None
                    m_n = INC(nsem, bi, 16)
                    if my("sync"):
                        W(e, nsem, m_n)
                        bi = e.dma_start(out=wdT_nat[oc * P:(oc + 1) * P, :],
                                         in_=wdtf_b[npar][:])
                    else:
                        bi = None
                    ntf_marks[oc] = (nsem, INC(nsem, bi, 16))
                for scn in range(8):
                    par = it % 2
                    gsem = "ga" if par == 0 else "gb"
                    if my("tensor"):
                        W(e, wlsem, ldw_marks[oc])
                        if it >= 2:
                            W(e, "act", act_marks[it - 2])
                        for t in range(NDT):
                            e.matmul(ps_g[par][:],
                                     wT_g[wpar][:, t * P:(t + 1) * P],
                                     xT_sb[:, t * S + scn * 512:t * S + (scn + 1) * 512],
                                     start=(t == 0), stop=(t == NDT - 1))
                        for t in range(NDT):
                            bi = e.matmul(ps_u[par][:],
                                          wT_u[wpar][:, t * P:(t + 1) * P],
                                          xT_sb[:, t * S + scn * 512:t * S + (scn + 1) * 512],
                                          start=(t == 0), stop=(t == NDT - 1))
                    else:
                        bi = None
                    pe_marks.append(INC("pe", bi))
                    if my("scalar"):
                        W(e, "pe", pe_marks[it])
                        if it >= 2:
                            W(e, "pool", pool_marks[it - 2])
                        e.activation(out=silu_b[par][:], in_=ps_g[par][:],
                                     func=AF.Silu)
                        bi = e.activation(out=up_b[par][:], in_=ps_u[par][:],
                                          func=AF.Identity)
                    else:
                        bi = None
                    act_marks.append(INC("act", bi))
                    if my("gpsimd"):
                        W(e, "act", act_marks[it])
                        if it >= 2:
                            W(e, gsem, st_marks[it - 2])
                        bi = e.tensor_tensor(out=h_b[par][:], in0=silu_b[par][:],
                                             in1=up_b[par][:], op=ALU.mult)
                    else:
                        bi = None
                    pool_marks.append(EDGE(my("gpsimd"), e, "pool", bi))
                    if my("gpsimd"):
                        bi = e.dma_start(
                            out=hT_s[oc][:, scn * 512:(scn + 1) * 512],
                            in_=h_b[par][:])
                    else:
                        bi = None
                    st_marks.append(INC(gsem, bi, 16))
                    it += 1
            if my("gpsimd"):
                W(e, "ga", C["ga"])
                W(e, "gb", C["gb"])
            if my("sync") and wdtf_b is not None:
                W(e, "na", C["na"])
                W(e, "nb", C["nb"])
        return ph

    def make_down_phase(x_res, dst, hT_sb, wdT_b, xsl, res_b, ps_d):
        def ph(en, e):
            my = lambda n: en == n
            peg_marks = {}
            dve_marks, st_marks, ldh_marks = [], [], []
            ld2_marks, ldx_marks = {}, {}
            for sc2 in range(4):
                if my("sync") and sc2 > 0:
                    W(e, "pe", peg_marks[(sc2 * 4 - 1, NOC - 1)])
                for ot in range(NOC):
                    bi = (e.dma_start(
                        out=hT_sb[:, ot * 1024:(ot + 1) * 1024],
                        in_=hT_s[ot][:, sc2 * 1024:(sc2 + 1) * 1024])
                        if my("sync") else None)
                    INC("lc", bi, 16)
                ldh_marks.append(C["lc"])
                for hc in range(4):
                    it2 = sc2 * 4 + hc
                    if my("sync") and it2 >= 1:
                        W(e, "dve", dve_marks[it2 - 1])
                    for st8 in range(8):
                        r0 = (sc2 * 8 + st8) * P
                        bi = (e.dma_start(
                            out=xsl[:, st8 * 512:(st8 + 1) * 512],
                            in_=x_res[r0:r0 + P, hc * 512:(hc + 1) * 512])
                            if my("sync") else None)
                        INC("la", bi, 16)
                    ldx_marks[it2] = C["la"]
                    for ot in range(NOC):
                        wpar = ot % 4
                        wsem = ("wa", "wb", "wc", "wd")[wpar]
                        if my("scalar"):
                            if (it2, ot - 4) in peg_marks:
                                W(e, "pe", peg_marks[(it2, ot - 4)])
                            elif ot < 4 and it2 > 0:
                                W(e, "pe", peg_marks[(it2 - 1, NOC - 1)])
                            bi = e.dma_start(
                                out=wdT_b[wpar][:],
                                in_=wdT_nat[ot * P:(ot + 1) * P,
                                            hc * 512:(hc + 1) * 512])
                        else:
                            bi = None
                        ld2_marks[(it2, ot)] = INC(wsem, bi, 16)
                        if my("tensor"):
                            W(e, wsem, ld2_marks[(it2, ot)])
                            if ot == 0:
                                W(e, "lc", ldh_marks[sc2])
                                if it2 >= 1:
                                    W(e, "dve", dve_marks[it2 - 1])
                            for st8 in range(8):
                                bi = e.matmul(
                                    ps_d[:, st8 * 512:(st8 + 1) * 512],
                                    hT_sb[:, ot * 1024 + st8 * P:ot * 1024 + (st8 + 1) * P],
                                    wdT_b[wpar][:],
                                    start=(ot == 0), stop=(ot == NOC - 1))
                        else:
                            bi = None
                        peg_marks[(it2, ot)] = INC("pe", bi)
                    if my("vector"):
                        W(e, "pe", peg_marks[(it2, NOC - 1)])
                        W(e, "la", ldx_marks[it2])
                        if it2 >= 1:
                            W(e, "ga", st_marks[it2 - 1])
                        for st8 in range(8):
                            bi = e.scalar_tensor_tensor(
                                out=res_b[:, st8 * 512:(st8 + 1) * 512],
                                in0=ps_d[:, st8 * 512:(st8 + 1) * 512],
                                scalar=0.5,
                                in1=xsl[:, st8 * 512:(st8 + 1) * 512],
                                op0=ALU.mult, op1=ALU.add)
                    else:
                        bi = None
                    dve_marks.append(INC("dve", bi))
                    if my("gpsimd"):
                        W(e, "dve", dve_marks[it2])
                    for st8 in range(8):
                        r0 = (sc2 * 8 + st8) * P
                        bi = (e.dma_start(
                            out=dst[r0:r0 + P, hc * 512:(hc + 1) * 512],
                            in_=res_b[:, st8 * 512:(st8 + 1) * 512])
                            if my("gpsimd") else None)
                        INC("ga", bi, 16)
                    st_marks.append(C["ga"])
            if my("gpsimd"):
                W(e, "ga", C["ga"])
        return ph

    def emit_ffn(tag, x_src, res_dst):
        es_x = ExitStack()
        xT_sb = es_x.enter_context(nc.sbuf_tensor(f"xT_{tag}", [P, NDT * S], BF16)).ap()

        es_a = ExitStack()
        xin = [es_a.enter_context(nc.sbuf_tensor(f"xin{i}_{tag}", [P, D], F32)).ap()
               for i in range(2)]
        sq_scr = es_a.enter_context(nc.sbuf_tensor(f"sq_{tag}", [P, D], BF16)).ap()
        xln = [es_a.enter_context(nc.sbuf_tensor(f"xln{i}_{tag}", [P, D], BF16)).ap()
               for i in range(2)]
        nsum_t = es_a.enter_context(nc.sbuf_tensor(f"nsum_{tag}", [P, NST], F32)).ap()
        nmean_t = es_a.enter_context(nc.sbuf_tensor(f"nmean_{tag}", [P, NST], F32)).ap()
        rstd_t = es_a.enter_context(nc.sbuf_tensor(f"rstd_{tag}", [P, NST], F32)).ap()
        sc1 = es_a.enter_context(nc.sbuf_tensor(f"sc1_{tag}", [P, 4], F32)).ap()
        sumsq_t = es_a.enter_context(nc.sbuf_tensor(f"ssq_{tag}", [P, NST], F32)).ap()
        var_t = es_a.enter_context(nc.sbuf_tensor(f"var_{tag}", [P, NST], F32)).ap()
        ps_tr = [es_a.enter_context(nc.psum_tensor(f"ps_tr{i}_{tag}", [P, 1024], BF16)).ap()
                 for i in range(2)]
        deq = None
        if tag == "f1":
            idx_b = [es_a.enter_context(nc.sbuf_tensor(f"idx{i}", [P, D], I32)).ap()
                     for i in range(2)]
            idx_bf = es_a.enter_context(nc.sbuf_tensor("idx_bf", [P, D], BF16)).ap()
            tcon = es_a.enter_context(nc.sbuf_tensor("tcon", [P, D], BF16)).ap()
            acc_b = [es_a.enter_context(nc.sbuf_tensor(f"acc{i}", [P, D], BF16)).ap()
                     for i in range(2)]
            deq = (idx_b, idx_bf, tcon, acc_b)
        run_block(f"ln_{tag}", make_ln_phase(
            x_src, xT_sb, (xin, sq_scr, xln, nsum_t, nmean_t, rstd_t, sc1,
                           sumsq_t, var_t), ps_tr, deq=deq))
        es_a.close()

        es_b = ExitStack()
        wT_g = [es_b.enter_context(nc.sbuf_tensor(f"wTg{i}_{tag}", [P, NDT * P], BF16)).ap()
                for i in range(2)]
        wT_u = [es_b.enter_context(nc.sbuf_tensor(f"wTu{i}_{tag}", [P, NDT * P], BF16)).ap()
                for i in range(2)]
        silu_b = [es_b.enter_context(nc.sbuf_tensor(f"si{i}_{tag}", [P, 512], BF16)).ap()
                  for i in range(2)]
        up_b = [es_b.enter_context(nc.sbuf_tensor(f"up{i}_{tag}", [P, 512], BF16)).ap()
                for i in range(2)]
        h_b = [es_b.enter_context(nc.sbuf_tensor(f"hb{i}_{tag}", [P, 512], BF16)).ap()
               for i in range(2)]
        ps_g = [es_b.enter_context(nc.psum_tensor(f"psg{i}_{tag}", [P, 512], F32)).ap()
                for i in range(2)]
        ps_u = [es_b.enter_context(nc.psum_tensor(f"psu{i}_{tag}", [P, 512], F32)).ap()
                for i in range(2)]
        wdtf_b = None
        if tag == "f1":
            wdtf_b = [es_b.enter_context(
                nc.sbuf_tensor(f"wdtf{i}", [P, D], BF16)).ap() for i in range(2)]
        run_block(f"gu_{tag}", make_gateup_phase(
            xT_sb, wT_g, wT_u, silu_b, up_b, h_b, ps_g, ps_u, wdtf_b=wdtf_b))
        es_b.close()
        es_x.close()

        es_c = ExitStack()
        hT_sb = es_c.enter_context(nc.sbuf_tensor(f"hTs_{tag}", [P, NOC * 1024], BF16)).ap()
        wdT_b = [es_c.enter_context(nc.sbuf_tensor(f"wdT{i}_{tag}", [P, 512], BF16)).ap()
                 for i in range(4)]
        xsl = es_c.enter_context(nc.sbuf_tensor(f"xsl_{tag}", [P, 8 * 512], F32)).ap()
        res_b = es_c.enter_context(nc.sbuf_tensor(f"res_{tag}", [P, 8 * 512], F32)).ap()
        ps_d = es_c.enter_context(nc.psum_tensor(f"psd_{tag}", [P, 4096], F32)).ap()
        run_block(f"dn_{tag}", make_down_phase(x_src, res_dst, hT_sb, wdT_b, xsl,
                                               res_b, ps_d))
        es_c.close()

    ffn1_dst = out if upto == "ffn1" else x1_s
    emit_ffn("f1", x_in, ffn1_dst)

    if upto == "ffn1":
        es.close()
        return nc

    es_x = ExitStack()
    xT_sb = es_x.enter_context(nc.sbuf_tensor("xT_at", [P, NDT * S], BF16)).ap()

    es_a = ExitStack()
    xin = [es_a.enter_context(nc.sbuf_tensor(f"xin{i}_at", [P, D], F32)).ap()
           for i in range(2)]
    sq_scr = es_a.enter_context(nc.sbuf_tensor("sq_at", [P, D], BF16)).ap()
    xln = [es_a.enter_context(nc.sbuf_tensor(f"xln{i}_at", [P, D], BF16)).ap()
           for i in range(2)]
    nsum_t = es_a.enter_context(nc.sbuf_tensor("nsum_at", [P, NST], F32)).ap()
    nmean_t = es_a.enter_context(nc.sbuf_tensor("nmean_at", [P, NST], F32)).ap()
    rstd_t = es_a.enter_context(nc.sbuf_tensor("rstd_at", [P, NST], F32)).ap()
    sc1 = es_a.enter_context(nc.sbuf_tensor("sc1_at", [P, 4], F32)).ap()
    sumsq_t = es_a.enter_context(nc.sbuf_tensor("ssq_at", [P, NST], F32)).ap()
    var_t = es_a.enter_context(nc.sbuf_tensor("var_at", [P, NST], F32)).ap()
    ps_tr = [es_a.enter_context(nc.psum_tensor(f"ps_tr{i}_at", [P, 1024], BF16)).ap()
             for i in range(2)]
    run_block("ln_at", make_ln_phase(
        x1_s, xT_sb, (xin, sq_scr, xln, nsum_t, nmean_t, rstd_t, sc1,
                      sumsq_t, var_t), ps_tr))
    es_a.close()

    es_p = ExitStack()
    cod_b = [es_p.enter_context(nc.sbuf_tensor(f"cod{i}", [P, D], F32)).ap()
             for i in range(2)]
    scb_b = [es_p.enter_context(nc.sbuf_tensor(f"scb{i}", [P, D], F32)).ap()
             for i in range(2)]
    w_bf = [es_p.enter_context(nc.sbuf_tensor(f"wbf{i}", [P, D], BF16)).ap()
            for i in range(2)]
    wv_bf = es_p.enter_context(nc.sbuf_tensor("wvbf", [P, NDT * 512], BF16)).ap()
    ev_b = [es_p.enter_context(nc.sbuf_tensor(f"ev{i}", [P, 512], BF16)).ap()
            for i in range(2)]
    ps_p = [es_p.enter_context(nc.psum_tensor(f"psp{i}", [P, 512], F32)).ap()
            for i in range(2)]

    def p3_proj(en, e):
        my = lambda n: en == n
        deq_marks, pe_marks, ev_marks, ld_marks, st_marks = [], [], [], [], []
        pit = 0
        for pj, (cd, sc, bT, dstT) in enumerate((
                (wcod["wq"], wscl["wq"], bqT, qT_sc),
                (wcod["wk"], wscl["wk"], bkT, kT_sc))):
            cd_r = cd.rearrange("(t p) f -> p t f", p=P)
            for fc in range(NDT):
                it3 = pj * NDT + fc
                par = it3 % 2
                lsem = "la" if par == 0 else "lb"
                if my("sync") and it3 >= 2:
                    W(e, "dve", deq_marks[it3 - 2])
                bi = (e.dma_start(out=cod_b[par][:],
                                  in_=cd_r[:, :, fc * P:(fc + 1) * P])
                      if my("sync") else None)
                INC(lsem, bi, 16)
                for t in range(NDT):
                    bi = (e.dma_start(
                        out=scb_b[par][:, t * P:(t + 1) * P],
                        in_=sc[4 * t:4 * t + 4, fc * P:(fc + 1) * P]
                        .unsqueeze(1).broadcast_to([4, 32, P]))
                        if my("sync") else None)
                    INC(lsem, bi, 16)
                ld_marks.append(C[lsem])
                if my("vector"):
                    W(e, lsem, ld_marks[it3])
                    if it3 >= 2:
                        W(e, "pe", pe_marks[(it3 - 2) * 8 + 7])
                    bi = e.tensor_tensor(out=w_bf[par][:], in0=cod_b[par][:],
                                         in1=scb_b[par][:], op=ALU.mult)
                else:
                    bi = None
                deq_marks.append(INC("dve", bi))
                for scn in range(8):
                    ppar = pit % 2
                    ssem = "ga" if ppar == 0 else "gb"
                    if my("tensor"):
                        W(e, "dve", deq_marks[it3])
                        if pit >= 2:
                            W(e, ev_marks[pit - 2][0], ev_marks[pit - 2][1])
                        for t in range(NDT):
                            bi = e.matmul(
                                ps_p[ppar][:],
                                w_bf[par][:, t * P:(t + 1) * P],
                                xT_sb[:, t * S + scn * 512:t * S + (scn + 1) * 512],
                                start=(t == 0), stop=(t == NDT - 1))
                    else:
                        bi = None
                    pe_marks.append(INC("pe", bi))
                    if my("scalar"):
                        W(e, "pe", pe_marks[pit])
                        if pit >= 2:
                            W(e, st_marks[pit - 2][0], st_marks[pit - 2][1])
                        bi = e.activation(out=ev_b[ppar][:], in_=ps_p[ppar][:],
                                          func=AF.Identity, bias=bT[:, fc:fc + 1])
                    else:
                        bi = None
                    ev_marks.append(("act", INC("act", bi)))
                    if my("gpsimd"):
                        W(e, "act", ev_marks[pit][1])
                        bi = e.dma_start(
                            out=dstT[fc * P:(fc + 1) * P,
                                     scn * 512:(scn + 1) * 512],
                            in_=ev_b[ppar][:])
                    else:
                        bi = None
                    st_marks.append((ssem, INC(ssem, bi, 16)))
                    pit += 1
        vdeq_marks = []
        cdv = wcod["wv"]
        scv = wscl["wv"]
        for fg in range(4):
            for t in range(NDT):
                vp = t % 2
                lsem = "la" if vp == 0 else "lb"
                if my("sync"):
                    if len(vdeq_marks) >= 2:
                        W(e, "dve", vdeq_marks[-2])
                    else:
                        W(e, "dve", deq_marks[30 + len(vdeq_marks)])
                bi = (e.dma_start(out=cod_b[vp][:, 0:512],
                                  in_=cdv[t * P:(t + 1) * P,
                                          fg * 512:(fg + 1) * 512])
                      if my("sync") else None)
                INC(lsem, bi, 16)
                bi = (e.dma_start(
                    out=scb_b[vp][:, 0:512],
                    in_=scv[4 * t:4 * t + 4, fg * 512:(fg + 1) * 512]
                    .unsqueeze(1).broadcast_to([4, 32, 512]))
                    if my("sync") else None)
                m_ld = INC(lsem, bi, 16)
                if my("vector"):
                    W(e, lsem, m_ld)
                    if fg > 0 and t == 0:
                        W(e, "pe", pe_marks[-1])
                    bi = e.tensor_tensor(out=wv_bf[:, t * 512:(t + 1) * 512],
                                         in0=cod_b[vp][:, 0:512],
                                         in1=scb_b[vp][:, 0:512], op=ALU.mult)
                else:
                    bi = None
                vdeq_marks.append(INC("dve", bi))
            dq_all = vdeq_marks[-1]
            for stl in range(NST):
                ppar = pit % 2
                ssem = "ga" if ppar == 0 else "gb"
                if my("tensor"):
                    W(e, "dve", dq_all)
                    if pit >= 2:
                        W(e, ev_marks[pit - 2][0], ev_marks[pit - 2][1])
                    for t in range(NDT):
                        bi = e.matmul(
                            ps_p[ppar][:],
                            xT_sb[:, t * S + stl * P:t * S + (stl + 1) * P],
                            wv_bf[:, t * 512:(t + 1) * 512],
                            start=(t == 0), stop=(t == NDT - 1))
                else:
                    bi = None
                pe_marks.append(INC("pe", bi))
                if my("vector"):
                    W(e, "pe", pe_marks[-1])
                    if pit >= 2:
                        W(e, st_marks[pit - 2][0], st_marks[pit - 2][1])
                    bi = e.scalar_tensor_tensor(
                        out=ev_b[ppar][:], in0=ps_p[ppar][:], scalar=1.0,
                        in1=bv_bc[:, fg * 512:(fg + 1) * 512],
                        op0=ALU.mult, op1=ALU.add)
                else:
                    bi = None
                ev_marks.append(("dve", INC("dve", bi)))
                if my("gpsimd"):
                    W(e, "dve", ev_marks[-1][1])
                for j in range(4):
                    bi = (e.dma_start(
                        out=v_sc[4 * fg + j][stl * P:(stl + 1) * P, :],
                        in_=ev_b[ppar][:, j * P:(j + 1) * P])
                        if my("gpsimd") else None)
                    INC(ssem, bi, 16)
                st_marks.append((ssem, C[ssem]))
                pit += 1
        if my("gpsimd"):
            W(e, "ga", C["ga"])
            W(e, "gb", C["gb"])

    run_block("p3_proj", p3_proj)
    es_p.close()
    es_x.close()

    if upto == "qkv":
        es.close()
        return nc

    es_t = ExitStack()
    wo_sb = es_t.enter_context(nc.sbuf_tensor("wo_sb", [P, NDT * D], BF16)).ap()
    aoT = es_t.enter_context(nc.sbuf_tensor("aoT", [P, NH * 512], BF16)).ap()
    kt_b = [es_t.enter_context(nc.sbuf_tensor(f"kt{i}", [P, S], BF16)).ap()
            for i in range(2)]
    v_b = [es_t.enter_context(nc.sbuf_tensor(f"vb{i}", [P, S], BF16)).ap()
           for i in range(2)]
    qt_b = [es_t.enter_context(nc.sbuf_tensor(f"qt{i}", [P, 512], BF16)).ap()
            for i in range(2)]
    exp_b = [es_t.enter_context(nc.sbuf_tensor(f"ex{i}", [P, 1024], BF16)).ap()
             for i in range(4)]
    avbf = [es_t.enter_context(nc.sbuf_tensor(f"avbf{i}", [P, 512], BF16)).ap()
            for i in range(2)]
    rbc = [es_t.enter_context(nc.sbuf_tensor(f"rbc{i}", [P, 512], BF16)).ap()
           for i in range(2)]
    rec_row = [es_t.enter_context(nc.sbuf_tensor(f"rec_row{i}", [1, 512], F32)).ap()
               for i in range(2)]
    codw = es_t.enter_context(nc.sbuf_tensor("codw", [P, D], F32)).ap()
    sclw = es_t.enter_context(nc.sbuf_tensor("sclw", [P, D], F32)).ap()
    x1_b = [es_t.enter_context(nc.sbuf_tensor(f"x1b{i}", [P, D], F32)).ap()
            for i in range(2)]
    xo_b = [es_t.enter_context(nc.sbuf_tensor(f"xob{i}", [P, D], F32)).ap()
            for i in range(2)]
    ps_s = [es_t.enter_context(nc.psum_tensor(f"pss{i}", [P, 1024], F32)).ap()
            for i in range(3)]
    ps_av = [es_t.enter_context(nc.psum_tensor(f"ps_av{i}", [P, 512], F32)).ap()
             for i in range(2)]

    att_dst = out if upto == "attn" else x2_s

    def p3_att(en, e):
        my = lambda n: en == n
        wodeq_marks = []
        cd_r = wcod["wo"].rearrange("(t p) f -> p t f", p=P)
        sc_r = wscl["wo"].rearrange("(t r) f -> r t f", r=4)
        for t in range(NDT):
            if my("sync") and t >= 1:
                W(e, "dve", wodeq_marks[t - 1])
            bi = (e.dma_start(out=codw[:], in_=cd_r[:, t, :])
                  if my("sync") else None)
            INC("lc", bi, 16)
            bi = (e.dma_start(out=sclw[:],
                              in_=sc_r[:, t, :].unsqueeze(1)
                              .broadcast_to([4, 32, D]))
                  if my("sync") else None)
            m_ld = INC("lc", bi, 16)
            if my("vector"):
                W(e, "lc", m_ld)
                bi = e.tensor_tensor(out=wo_sb[:, t * D:(t + 1) * D],
                                     in0=codw[:], in1=sclw[:], op=ALU.mult)
            else:
                bi = None
            wodeq_marks.append(INC("dve", bi))
        wo_ready = wodeq_marks[-1]

        sc_marks, ex_marks, av_marks = {}, {}, {}
        ao_marks, rec_marks, rbc_marks, avc_marks, ld3_marks = {}, {}, {}, {}, {}
        woe_marks, wost_marks, wo_end, ldx_marks = {}, {}, {}, {}

        for qsc in range(8):
            for h in range(NH):
                ih = qsc * NH + h
                par = ih % 2
                bA = ps_av[ih % 2]
                bD = ps_av[(ih + 1) % 2]
                lsem = "la" if par == 0 else "lb"
                if my("sync") and ih >= 2:
                    W(e, "pe", av_marks[(ih - 2) * 16 + 15])
                bi = (e.dma_start(out=kt_b[par][:],
                                  in_=kT_sc[h * P:(h + 1) * P, :])
                      if my("sync") else None)
                INC(lsem, bi, 16)
                bi = (e.dma_start(out=v_b[par][:],
                                  in_=v_sc[h].rearrange("(kt p) d -> p kt d", p=P))
                      if my("sync") else None)
                INC(lsem, bi, 16)
                bi = (e.dma_start(out=qt_b[par][:],
                                  in_=qT_sc[h * P:(h + 1) * P,
                                            qsc * 512:(qsc + 1) * 512])
                      if my("sync") else None)
                ld3_marks[ih] = INC(lsem, bi, 16)
                for gg in range(18):
                    if gg < 16:
                        grp = gg
                        gctr = ih * 16 + grp
                        spar = gctr % 3
                        epar = gctr % 4
                        if my("tensor"):
                            if grp == 0:
                                W(e, lsem, ld3_marks[ih])
                                if h == 0 and qsc >= 1:
                                    W(e, "pe", wo_end[qsc - 1])
                                    W(e, "dve", woe_marks[(qsc - 1, 3, 3)])
                            if gctr >= 3:
                                W(e, "act", ex_marks[gctr - 3])
                            for i in range(2):
                                kt = 2 * grp + i
                                bi = e.matmul(ps_s[spar][:, i * 512:(i + 1) * 512],
                                              kt_b[par][:, kt * P:(kt + 1) * P],
                                              qt_b[par][:], start=True, stop=True)
                        else:
                            bi = None
                        sc_marks[gctr] = INC("pe", bi)
                        if my("scalar"):
                            W(e, "pe", sc_marks[gctr])
                            if gctr >= 4:
                                W(e, "pe", av_marks[gctr - 4])
                            bi = e.activation(out=exp_b[epar][:], in_=ps_s[spar][:],
                                              func=AF.Exp, bias=mbias[:, 0:1],
                                              scale=INV_SQRT_HD)
                        else:
                            bi = None
                        ex_marks[gctr] = INC("act", bi)
                    if gg >= 2:
                        grp = gg - 2
                        gctr = ih * 16 + grp
                        epar = gctr % 4
                        if my("tensor"):
                            W(e, "act", ex_marks[gctr])
                            if grp == 0 and ih >= 1:
                                W(e, "dve", rec_marks[ih - 1])
                                W(e, "dve", avc_marks[ih - 1])
                            for i in range(2):
                                kt = 2 * grp + i
                                e.matmul(bA[:],
                                         v_b[par][:, kt * P:(kt + 1) * P],
                                         exp_b[epar][:, i * 512:(i + 1) * 512],
                                         start=(kt == 0), stop=(kt == 31))
                                bi = e.matmul(bD[0:1, :], ones_col[:],
                                              exp_b[epar][:, i * 512:(i + 1) * 512],
                                              start=(kt == 0), stop=(kt == 31))
                        else:
                            bi = None
                        av_marks[gctr] = INC("pe", bi)
                if my("vector"):
                    W(e, "pe", av_marks[ih * 16 + 15])
                    if ih >= 2:
                        W(e, "gc", rbc_marks[ih - 2])
                    bi = e.reciprocal(out=rec_row[ih % 2][:], in_=bD[0:1, :])
                else:
                    bi = None
                rec_marks[ih] = INC("dve", bi)
                tpar = ih % 2
                if my("gpsimd"):
                    W(e, "dve", rec_marks[ih])
                    if ih >= 2:
                        W(e, "dve", ao_marks[ih - 2])
                    bi = e.dma_start(out=rec_dr[tpar], in_=rec_row[tpar][:])
                else:
                    bi = None
                m_gc = INC("gc", bi, 16)
                if my("gpsimd"):
                    W(e, "gc", m_gc)
                    bi = e.dma_start(out=rbc[tpar][:],
                                     in_=rec_dr[tpar].broadcast_to([P, 512]))
                else:
                    bi = None
                rbc_marks[ih] = INC("gc", bi, 16)
                if my("vector"):
                    if h == 0 and qsc >= 1:
                        W(e, "pe", wo_end[qsc - 1])
                    if ih >= 2:
                        W(e, "dve", ao_marks[ih - 2])
                    bi = e.tensor_copy(out=avbf[tpar][:], in_=bA[:])
                else:
                    bi = None
                avc_marks[ih] = INC("dve", bi)
                if my("vector"):
                    W(e, "dve", avc_marks[ih])
                    W(e, "gc", rbc_marks[ih])
                    bi = e.tensor_tensor(out=aoT[:, h * 512:(h + 1) * 512],
                                         in0=avbf[tpar][:], in1=rbc[tpar][:],
                                         op=ALU.mult)
                else:
                    bi = None
                ao_marks[ih] = INC("dve", bi)
            for stl in range(4):
                spar2 = stl % 2
                wsem = "wa" if (qsc * 4 + stl) % 2 == 0 else "wb"
                r0 = (qsc * 4 + stl) * P
                if my("sync"):
                    prev = qsc * 4 + stl - 2
                    if prev >= 0:
                        W(e, "dve", woe_marks[(prev // 4, prev % 4, 3)])
                    bi = e.dma_start(out=x1_b[spar2][:], in_=x1_s[r0:r0 + P, :])
                else:
                    bi = None
                ldx_marks[(qsc, stl)] = INC(wsem, bi, 16)
                for dmn in range(4):
                    widx = stl * 4 + dmn
                    wps = ps_s[widx % 2][:, 0:512]
                    if my("tensor"):
                        if stl == 0 and dmn == 0:
                            W(e, "dve", ao_marks[qsc * NH + NH - 1])
                            W(e, "dve", wo_ready)
                            W(e, "act", ex_marks[qsc * NH * 16 + NH * 16 - 1])
                        prev = (qsc * 16 + widx) - 2
                        if prev >= 0:
                            pq, prem = divmod(prev, 16)
                            W(e, "dve", woe_marks[(pq, prem // 4, prem % 4)])
                        for t in range(NDT):
                            bi = e.matmul(
                                wps,
                                aoT[:, t * 512 + stl * P:t * 512 + (stl + 1) * P],
                                wo_sb[:, t * D + dmn * 512:t * D + (dmn + 1) * 512],
                                start=(t == 0), stop=(t == NDT - 1))
                    else:
                        bi = None
                    m_pe = INC("pe", bi)
                    if dmn == 3:
                        wo_end[qsc] = m_pe
                    if my("vector"):
                        W(e, "pe", m_pe)
                        W(e, wsem, ldx_marks[(qsc, stl)])
                        prev2 = qsc * 4 + stl - 2
                        if prev2 >= 0:
                            W(e, wost_marks[(prev2 // 4, prev2 % 4)][0],
                              wost_marks[(prev2 // 4, prev2 % 4)][1])
                        bi = e.scalar_tensor_tensor(
                            out=xo_b[spar2][:, dmn * 512:(dmn + 1) * 512],
                            in0=wps, scalar=1.0,
                            in1=bo_bc[:, dmn * 512:(dmn + 1) * 512],
                            op0=ALU.mult, op1=ALU.add)
                    else:
                        bi = None
                    EDGE(my("vector"), e, "dve", bi)
                    bi = (e.tensor_tensor(
                        out=xo_b[spar2][:, dmn * 512:(dmn + 1) * 512],
                        in0=xo_b[spar2][:, dmn * 512:(dmn + 1) * 512],
                        in1=x1_b[spar2][:, dmn * 512:(dmn + 1) * 512],
                        op=ALU.add) if my("vector") else None)
                    woe_marks[(qsc, stl, dmn)] = INC("dve", bi)
                ssem = "ga" if spar2 == 0 else "gb"
                if my("gpsimd"):
                    W(e, "dve", woe_marks[(qsc, stl, 3)])
                    bi = e.dma_start(out=att_dst[r0:r0 + P, :], in_=xo_b[spar2][:])
                else:
                    bi = None
                wost_marks[(qsc, stl)] = (ssem, INC(ssem, bi, 16))
        if my("gpsimd"):
            W(e, "ga", C["ga"])
            W(e, "gb", C["gb"])

    run_block("p3_att", p3_att)
    es_t.close()

    if upto == "attn":
        es.close()
        return nc

    emit_ffn("f2", x2_s, out)

    es.close()
    return nc


def kernel(**inputs):
    x = np.asarray(inputs["L0_attn_x"])
    lut = np.asarray(inputs["L0_ffn_lut"]).reshape(1, 16)
    gate_idx = np.asarray(inputs["L0_ffn_gate_idx"])
    up_idx = np.asarray(inputs["L0_ffn_up_idx"])
    down_idx = np.asarray(inputs["L0_ffn_down_idx"])
    gate_pad = np.vstack([gate_idx, gate_idx[:6144 - IFF]])
    up_pad = np.vstack([up_idx, up_idx[:6144 - IFF]])
    common = {
        "lut": np.ascontiguousarray(lut),
        "wq_codes": np.ascontiguousarray(inputs["L0_attn_wq_codes"]),
        "wk_codes": np.ascontiguousarray(inputs["L0_attn_wk_codes"]),
        "wv_codes": np.ascontiguousarray(inputs["L0_attn_wv_codes"]),
        "wo_codes": np.ascontiguousarray(inputs["L0_attn_wo_codes"]),
        "wq_scales": np.ascontiguousarray(inputs["L0_attn_wq_scales"]),
        "wk_scales": np.ascontiguousarray(inputs["L0_attn_wk_scales"]),
        "wv_scales": np.ascontiguousarray(inputs["L0_attn_wv_scales"]),
        "wo_scales": np.ascontiguousarray(inputs["L0_attn_wo_scales"]),
        "bq": np.ascontiguousarray(np.asarray(inputs["L0_attn_bq"]).reshape(1, D)),
        "bk": np.ascontiguousarray(np.asarray(inputs["L0_attn_bk"]).reshape(1, D)),
        "bv": np.ascontiguousarray(np.asarray(inputs["L0_attn_bv"]).reshape(1, D)),
        "bo": np.ascontiguousarray(np.asarray(inputs["L0_attn_bo"]).reshape(1, D)),
    }
    nc = build("full")
    in_maps = [dict(common,
                    x=np.ascontiguousarray(x[b]),
                    gate_sh=np.ascontiguousarray(gate_pad[768 * b:768 * (b + 1)]),
                    up_sh=np.ascontiguousarray(up_pad[768 * b:768 * (b + 1)]),
                    down_sh=np.ascontiguousarray(down_idx[256 * b:256 * (b + 1)]))
               for b in range(8)]
    res = run_bass_kernel_spmd(nc, in_maps, core_ids=list(range(8)))
    return np.stack([np.asarray(r["out"], dtype=np.float32)
                     for r in res.results], axis=0)


# version 12
# speedup vs baseline: 1.0133x; 1.0133x over previous
import math
import numpy as np
import ml_dtypes
from contextlib import ExitStack

import concourse.bass as bass
import concourse.mybir as mybir
from concourse.bass_utils import run_bass_kernel_spmd

F32 = mybir.dt.float32
BF16 = mybir.dt.bfloat16
I32 = mybir.dt.int32
AF = mybir.ActivationFunctionType
ALU = mybir.AluOpType
AXL = mybir.AxisListType

P = 128
S = 4096
D = 2048
NH = 16
HD = 128
IFF = 5504
NOC = IFF // P
NST = S // P
NDT = D // P
INV_SQRT_HD = 1.0 / math.sqrt(HD)
EXP_SHIFT = -25.0
LN_EPS = 1e-5

ENGS = ("sync", "tensor", "vector", "scalar", "gpsimd")
OGROUPS = [(g * 512, min(512, IFF - g * 512)) for g in range((IFF + 511) // 512)]
SEM_NAMES = ("bar", "pe", "dve", "act", "pool",
             "la", "lb", "lc", "ld4", "sa", "sb", "wa", "wb", "wc", "wd", "ga", "gb", "gc",
             "ca", "cb", "cc", "na", "nb")


def build(upto="full"):
    nc = bass.Bass("TRN2", target_bir_lowering=False)

    x_in = nc.dram_tensor("x", [S, D], F32, kind="ExternalInput").ap()
    gate_sh = nc.dram_tensor("gate_sh", [768, D], I32, kind="ExternalInput").ap()
    up_sh = nc.dram_tensor("up_sh", [768, D], I32, kind="ExternalInput").ap()
    down_sh = nc.dram_tensor("down_sh", [256, IFF], I32, kind="ExternalInput").ap()
    lut_in = nc.dram_tensor("lut", [1, 16], F32, kind="ExternalInput").ap()
    wcod = {}
    wscl = {}
    for nm in ("wq", "wk", "wv", "wo"):
        wcod[nm] = nc.dram_tensor(nm + "_codes", [D, D], F32, kind="ExternalInput").ap()
        wscl[nm] = nc.dram_tensor(nm + "_scales", [64, D], F32, kind="ExternalInput").ap()
    bq_in = nc.dram_tensor("bq", [1, D], F32, kind="ExternalInput").ap()
    bk_in = nc.dram_tensor("bk", [1, D], F32, kind="ExternalInput").ap()
    bv_in = nc.dram_tensor("bv", [1, D], F32, kind="ExternalInput").ap()
    bo_in = nc.dram_tensor("bo", [1, D], F32, kind="ExternalInput").ap()
    out = nc.dram_tensor("out", [S, D], F32, kind="ExternalOutput").ap()

    wg_shd = nc.dram_tensor("wg_shd", [768, D], BF16).ap()
    wu_shd = nc.dram_tensor("wu_shd", [768, D], BF16).ap()
    wd_shd = nc.dram_tensor("wd_shd", [256, IFF], BF16).ap()
    wg_nat = nc.dram_tensor("wg_nat", [6144, D], BF16, addr_space="Shared").ap()
    wu_nat = nc.dram_tensor("wu_nat", [6144, D], BF16, addr_space="Shared").ap()
    wd_nat = nc.dram_tensor("wd_nat", [D, IFF], BF16, addr_space="Shared").ap()
    wdT_nat = nc.dram_tensor("wdT_nat", [IFF, D], BF16).ap()
    hT_s = nc.dram_tensor("hT_s", [NOC, P, S], BF16).ap()
    x1_s = nc.dram_tensor("x1_s", [S, D], F32).ap()
    x2_s = nc.dram_tensor("x2_s", [S, D], F32).ap()
    qT_sc = nc.dram_tensor("qT_sc", [D, S], BF16).ap()
    rec_dr = [nc.dram_tensor(f"rec_dr{i}", [1, 512], F32).ap() for i in range(2)]
    kT_sc = nc.dram_tensor("kT_sc", [D, S], BF16).ap()
    v_sc = nc.dram_tensor("v_sc", [NH, S, HD], BF16).ap()

    es = ExitStack()

    ident = nc.alloc_sbuf_tensor("ident", [P, P], BF16).ap()
    ones_row = nc.alloc_sbuf_tensor("ones_row", [1, P], F32).ap()
    ones_col = nc.alloc_sbuf_tensor("ones_col", [P, 1], BF16).ap()
    one_one = nc.alloc_sbuf_tensor("one_one", [1, 1], F32).ap()
    lut_f = nc.alloc_sbuf_tensor("lut_f", [P, 16], F32).ap()
    mbias = nc.alloc_sbuf_tensor("mbias", [P, 1], F32).ap()
    eps_t = nc.alloc_sbuf_tensor("eps_t", [P, 1], F32).ap()
    bqT = nc.alloc_sbuf_tensor("bqT", [P, NDT], F32).ap()
    bkT = nc.alloc_sbuf_tensor("bkT", [P, NDT], F32).ap()
    bo_bc = nc.alloc_sbuf_tensor("bo_bc", [P, D], F32).ap()
    bv_bc = nc.alloc_sbuf_tensor("bv_bc", [P, D], F32).ap()

    sems = {k: es.enter_context(nc.semaphore(k)) for k in SEM_NAMES}
    C = {k: 0 for k in SEM_NAMES}
    NBLK = [0]

    def run_block(name, phase_fn):
        snap = dict(C)
        results = []
        with nc.Block(name) as blk:
            for ename in ENGS:
                C.clear()
                C.update(snap)

                def body(e, ename=ename):
                    if NBLK[0] > 0:
                        e.wait_ge(sems["bar"], 5 * NBLK[0])
                    phase_fn(ename, e)
                    e.nop().then_inc(sems["bar"], 1)

                getattr(blk, ename)(body)
                results.append(dict(C))
        for r in results[1:]:
            assert r == results[0], (name, r, results[0])
        C.clear()
        C.update(results[0])
        NBLK[0] += 1

    def INC(sem, bi, k=1):
        C[sem] += k
        if bi is not None:
            bi.then_inc(sems[sem], k)
        return C[sem]

    def W(e, sem, val):
        e.wait_ge(sems[sem], val)

    def EDGE(on, e, sem, bi):
        m = INC(sem, bi)
        if on:
            e.wait_ge(sems[sem], m)
        return m

    es0 = ExitStack()
    b_rows = es0.enter_context(nc.sbuf_tensor("b_rows", [1, 4 * D], F32)).ap()
    ps_init = es0.enter_context(nc.psum_tensor("ps_init", [P, 2 * NDT], F32)).ap()
    ps_bb = es0.enter_context(nc.psum_tensor("ps_bb", [P, D], F32)).ap()

    def p0(en, e):
        my = lambda n: en == n
        for dst, src in ((lut_f[:], lut_in.broadcast_to([P, 16])),
                         (b_rows[:, 0:D], bq_in), (b_rows[:, D:2 * D], bk_in),
                         (b_rows[:, 2 * D:3 * D], bv_in),
                         (b_rows[:, 3 * D:4 * D], bo_in)):
            INC("la", e.dma_start(out=dst, in_=src) if my("sync") else None, 16)
        ld_all = C["la"]
        bi = e.memset(ident[:], 0.0) if my("gpsimd") else None
        mz = INC("pool", bi)
        if my("gpsimd"):
            e.memset(ones_row[:], 1.0)
            e.memset(ones_col[:], 1.0)
            e.memset(one_one[:], 1.0)
            e.memset(mbias[:], EXP_SHIFT)
            e.memset(eps_t[:], LN_EPS)
            e.wait_ge(sems["pool"], mz)
            bi = e.affine_select(out=ident[:], in_=ident[:],
                                 compare_op=ALU.not_equal, fill=1.0, base=0,
                                 pattern=[[-1, P]], channel_multiplier=1)
        else:
            bi = None
        pool_done = INC("pool", bi)

        if my("tensor"):
            W(e, "la", ld_all)
            W(e, "pool", pool_done)
        for j, base in enumerate((0, D)):
            for fc in range(NDT):
                bi = (e.matmul(ps_init[:, j * NDT + fc:j * NDT + fc + 1],
                               b_rows[:, base + fc * P:base + (fc + 1) * P],
                               one_one[:], start=True, stop=True)
                      if my("tensor") else None)
        pe_bT = INC("pe", bi)
        for c in range(4):
            bi = (e.matmul(ps_bb[:, c * 512:(c + 1) * 512], ones_row[:],
                           b_rows[:, 3 * D + c * 512:3 * D + (c + 1) * 512],
                           start=True, stop=True) if my("tensor") else None)
        pe_bo = INC("pe", bi)
        if my("vector"):
            W(e, "pe", pe_bo)
            e.tensor_copy(out=bqT[:], in_=ps_init[:, 0:NDT])
            e.tensor_copy(out=bkT[:], in_=ps_init[:, NDT:2 * NDT])
            bi = e.tensor_copy(out=bo_bc[:], in_=ps_bb[:])
        else:
            bi = None
        dve_bo = INC("dve", bi)
        if my("tensor"):
            W(e, "dve", dve_bo)
        for c in range(4):
            bi = (e.matmul(ps_bb[:, c * 512:(c + 1) * 512], ones_row[:],
                           b_rows[:, 2 * D + c * 512:2 * D + (c + 1) * 512],
                           start=True, stop=True) if my("tensor") else None)
        pe_bv = INC("pe", bi)
        if my("vector"):
            W(e, "pe", pe_bv)
            bi = e.tensor_copy(out=bv_bc[:], in_=ps_bb[:])
        else:
            bi = None
        INC("dve", bi)

    run_block("p0", p0)
    es0.close()

    def make_ln_phase(x_src, xT_dst, bufs, ps_tr, deq=None):
        xin, sq_scr, xln, nsum_t, nmean_t, rstd_t, sc1, sumsq_t, var_t = bufs
        xT_v = xT_dst.rearrange("p (t s) -> p t s", t=NDT)
        if deq is not None:
            idx_b, idx_bf, tcon, acc_b = deq
            deq_units = ([(gate_sh[c * P:(c + 1) * P, :],
                           wg_shd[c * P:(c + 1) * P, :], D) for c in range(6)] +
                         [(up_sh[c * P:(c + 1) * P, :],
                           wu_shd[c * P:(c + 1) * P, :], D) for c in range(6)])
            deq_units_wd = [(down_sh[c * P:(c + 1) * P, c0:min(c0 + D, IFF)],
                             wd_shd[c * P:(c + 1) * P, c0:min(c0 + D, IFF)],
                             min(c0 + D, IFF) - c0)
                            for c in range(2) for c0 in range(0, IFF, D)]

        def ph(en, e):
            my = lambda n: en == n
            dve_marks, ld_marks = [], []
            dqi = [0]
            dq_st_marks = {}
            dq_cast_marks = {}

            def emit_deq(srcap, dstap, Wd_):
                i = dqi[0]
                par = i % 2
                lsem = "lc" if par == 0 else "ld4"
                ssem = "sa" if par == 0 else "sb"
                if my("sync"):
                    if i >= 2:
                        W(e, "dve", dq_cast_marks[i - 2])
                    bi = e.dma_start(out=idx_b[par][:, 0:Wd_], in_=srcap)
                else:
                    bi = None
                m_ld = INC(lsem, bi, 16)
                if my("vector"):
                    W(e, lsem, m_ld)
                    bi = e.tensor_copy(out=idx_bf[:, 0:Wd_],
                                       in_=idx_b[par][:, 0:Wd_])
                else:
                    bi = None
                dq_cast_marks[i] = EDGE(my("vector"), e, "dve", bi)
                if my("vector") and i >= 2:
                    W(e, dq_st_marks[i - 2][0], dq_st_marks[i - 2][1])
                bi = (e.tensor_scalar(out=acc_b[par][:, 0:Wd_],
                                      in0=idx_bf[:, 0:Wd_],
                                      scalar1=0.0, scalar2=lut_f[:, 0:1],
                                      op0=ALU.is_equal, op1=ALU.mult)
                      if my("vector") else None)
                EDGE(my("vector"), e, "dve", bi)
                for j in range(1, 16):
                    bi = (e.tensor_scalar(out=tcon[:, 0:Wd_],
                                          in0=idx_bf[:, 0:Wd_],
                                          scalar1=float(j),
                                          scalar2=lut_f[:, j:j + 1],
                                          op0=ALU.is_equal, op1=ALU.mult)
                          if my("vector") else None)
                    EDGE(my("vector"), e, "dve", bi)
                    bi = (e.tensor_tensor(out=acc_b[par][:, 0:Wd_],
                                          in0=acc_b[par][:, 0:Wd_],
                                          in1=tcon[:, 0:Wd_], op=ALU.add)
                          if my("vector") else None)
                    m_dve = EDGE(my("vector"), e, "dve", bi)
                if my("scalar"):
                    W(e, "dve", m_dve)
                    bi = e.dma_start(out=dstap, in_=acc_b[par][:, 0:Wd_])
                else:
                    bi = None
                dq_st_marks[i] = (ssem, INC(ssem, bi, 16))
                dqi[0] += 1
            for st in range(NST):
                par = st % 2
                lsem = "la" if par == 0 else "lb"
                dm, am, pm = [], [], []
                if my("sync"):
                    if st >= 2:
                        W(e, "dve", dve_marks[st - 2][4])
                    bi = e.dma_start(out=xin[par][:],
                                     in_=x_src[st * P:(st + 1) * P, :])
                else:
                    bi = None
                ld_marks.append(INC(lsem, bi, 16))
                if my("vector"):
                    W(e, lsem, ld_marks[st])
                    bi = e.reduce_sum(out=nsum_t[:, st:st + 1], in_=xin[par][:],
                                      axis=AXL.X, negate=True)
                else:
                    bi = None
                dm.append(INC("dve", bi))
                if my("scalar"):
                    W(e, lsem, ld_marks[st])
                    bi = e.activation(out=sq_scr[:], in_=xin[par][:],
                                      func=AF.Square,
                                      accum_out=sumsq_t[:, st:st + 1])
                else:
                    bi = None
                am.append(INC("act", bi))
                if my("vector"):
                    W(e, "dve", dm[0])
                    bi = e.tensor_scalar(out=nmean_t[:, st:st + 1],
                                         in0=nsum_t[:, st:st + 1],
                                         scalar1=1.0 / D, scalar2=None,
                                         op0=ALU.mult)
                else:
                    bi = None
                dm.append(INC("dve", bi))
                if my("vector"):
                    W(e, "dve", dm[1])
                    bi = e.tensor_scalar(out=sc1[:, 1:2],
                                         in0=nmean_t[:, st:st + 1],
                                         scalar1=nmean_t[:, st:st + 1],
                                         scalar2=None, op0=ALU.mult)
                else:
                    bi = None
                dm.append(INC("dve", bi))
                if my("vector"):
                    W(e, "dve", dm[2])
                    W(e, "act", am[0])
                    bi = e.scalar_tensor_tensor(out=var_t[:, st:st + 1],
                                                in0=sumsq_t[:, st:st + 1],
                                                scalar=1.0 / D, in1=sc1[:, 1:2],
                                                op0=ALU.mult, op1=ALU.subtract)
                else:
                    bi = None
                dm.append(INC("dve", bi))
                if my("scalar"):
                    W(e, "dve", dm[3])
                    bi = e.activation(out=sc1[:, 3:4], in_=var_t[:, st:st + 1],
                                      func=AF.Ln, bias=eps_t[:, 0:1])
                else:
                    bi = None
                am.append(INC("act", bi))
                if my("scalar"):
                    W(e, "act", am[1])
                    bi = e.activation(out=rstd_t[:, st:st + 1], in_=sc1[:, 3:4],
                                      func=AF.Exp, scale=-0.5)
                else:
                    bi = None
                am.append(INC("act", bi))
                if my("vector"):
                    W(e, "act", am[2])
                    bi = e.tensor_scalar(out=xln[par][:], in0=xin[par][:],
                                         scalar1=nmean_t[:, st:st + 1],
                                         scalar2=rstd_t[:, st:st + 1],
                                         op0=ALU.add, op1=ALU.mult)
                else:
                    bi = None
                dm.append(INC("dve", bi))
                for half in range(2):
                    if my("tensor"):
                        W(e, "dve", dm[4])
                        if st >= 1:
                            W(e, "dve", dve_marks[st - 1][5 + half])
                        for b in range(8):
                            ht = half * 8 + b
                            bi = e.matmul(
                                ps_tr[half][:, b * P:(b + 1) * P],
                                xln[par][:, ht * P:(ht + 1) * P], ident[:],
                                is_transpose=True, start=(b == 0), stop=(b == 7))
                    else:
                        bi = None
                    pm.append(INC("pe", bi))
                for half in range(2):
                    if my("vector"):
                        W(e, "pe", pm[half])
                        bi = e.tensor_copy(
                            out=xT_v[:, half * 8:(half + 1) * 8,
                                     st * P:(st + 1) * P],
                            in_=ps_tr[half].rearrange("p (b c) -> p b c", b=8))
                    else:
                        bi = None
                    dm.append(INC("dve", bi))
                dve_marks.append(dm)
            if deq is not None:
                for srcap, dstap, Wd_ in deq_units:
                    emit_deq(srcap, dstap, Wd_)
                gu_sa, gu_sb = C["sa"], C["sb"]
                for srcap, dstap, Wd_ in deq_units_wd:
                    emit_deq(srcap, dstap, Wd_)
                if my("gpsimd"):
                    W(e, "sa", gu_sa)
                    W(e, "sb", gu_sb)
                    bi = e.collective_compute(
                        "AllGather", ALU.bypass,
                        replica_groups=[list(range(8))],
                        ins=[wg_shd], outs=[wg_nat])
                else:
                    bi = None
                INC("ca", bi, 1)
                bi = (e.collective_compute(
                    "AllGather", ALU.bypass,
                    replica_groups=[list(range(8))],
                    ins=[wu_shd], outs=[wu_nat]) if my("gpsimd") else None)
                INC("cb", bi, 1)
                if my("gpsimd"):
                    W(e, "sa", C["sa"])
                    W(e, "sb", C["sb"])
                    bi = e.collective_compute(
                        "AllGather", ALU.bypass,
                        replica_groups=[list(range(8))],
                        ins=[wd_shd], outs=[wd_nat])
                else:
                    bi = None
                INC("cc", bi, 1)
        return ph

    def make_gateup_phase(xT_sb, wT_g, wT_u, silu_b, up_b, h_b, ps_g, ps_u,
                          wdtf_b=None):
        def ph(en, e):
            my = lambda n: en == n
            it = 0
            pe_marks, act_marks, pool_marks, st_marks, ldw_marks = [], [], [], [], []
            ntf_marks = {}
            for oc in range(NOC):
                wpar = oc % 2
                wlsem = "lc" if wpar == 0 else "ld4"
                if my("sync"):
                    if oc >= 2:
                        W(e, "pe", pe_marks[(oc - 2) * 8 + 7])
                    if wdtf_b is not None and oc == 0:
                        W(e, "ca", C["ca"])
                        W(e, "cb", C["cb"])
                for ht in range(NDT):
                    bi = (e.dma_start(
                        out=wT_g[wpar][ht][:],
                        in_=wg_nat[oc * P:(oc + 1) * P, ht * P:(ht + 1) * P],
                        transpose=True) if my("sync") else None)
                    INC(wlsem, bi, 16)
                    bi = (e.dma_start(
                        out=wT_u[wpar][ht][:],
                        in_=wu_nat[oc * P:(oc + 1) * P, ht * P:(ht + 1) * P],
                        transpose=True) if my("sync") else None)
                    INC(wlsem, bi, 16)
                ldw_marks.append(C[wlsem])
                if wdtf_b is not None:
                    npar = oc % 2
                    nsem = "na" if npar == 0 else "nb"
                    if my("sync"):
                        if oc == 0:
                            W(e, "cc", C["cc"])
                        if oc >= 2:
                            W(e, nsem, ntf_marks[oc - 2][1])
                        bi = e.dma_start(out=wdtf_b[npar][:],
                                         in_=wd_nat[:, oc * P:(oc + 1) * P],
                                         transpose=True)
                    else:
                        bi = None
                    m_n = INC(nsem, bi, 16)
                    if my("sync"):
                        W(e, nsem, m_n)
                        bi = e.dma_start(out=wdT_nat[oc * P:(oc + 1) * P, :],
                                         in_=wdtf_b[npar][:])
                    else:
                        bi = None
                    ntf_marks[oc] = (nsem, INC(nsem, bi, 16))
                for scn in range(8):
                    par = it % 2
                    gsem = "ga" if par == 0 else "gb"
                    if my("tensor"):
                        W(e, wlsem, ldw_marks[oc])
                        if it >= 2:
                            W(e, "act", act_marks[it - 2])
                        for t in range(NDT):
                            e.matmul(ps_g[par][:],
                                     wT_g[wpar][t][:],
                                     xT_sb[:, t * S + scn * 512:t * S + (scn + 1) * 512],
                                     start=(t == 0), stop=(t == NDT - 1))
                        for t in range(NDT):
                            bi = e.matmul(ps_u[par][:],
                                          wT_u[wpar][t][:],
                                          xT_sb[:, t * S + scn * 512:t * S + (scn + 1) * 512],
                                          start=(t == 0), stop=(t == NDT - 1))
                    else:
                        bi = None
                    pe_marks.append(INC("pe", bi))
                    if my("scalar"):
                        W(e, "pe", pe_marks[it])
                        if it >= 2:
                            W(e, "pool", pool_marks[it - 2])
                        e.activation(out=silu_b[par][:], in_=ps_g[par][:],
                                     func=AF.Silu)
                        bi = e.activation(out=up_b[par][:], in_=ps_u[par][:],
                                          func=AF.Identity)
                    else:
                        bi = None
                    act_marks.append(INC("act", bi))
                    if my("gpsimd"):
                        W(e, "act", act_marks[it])
                        if it >= 2:
                            W(e, gsem, st_marks[it - 2])
                        bi = e.tensor_tensor(out=h_b[par][:], in0=silu_b[par][:],
                                             in1=up_b[par][:], op=ALU.mult)
                    else:
                        bi = None
                    pool_marks.append(EDGE(my("gpsimd"), e, "pool", bi))
                    if my("gpsimd"):
                        bi = e.dma_start(
                            out=hT_s[oc][:, scn * 512:(scn + 1) * 512],
                            in_=h_b[par][:])
                    else:
                        bi = None
                    st_marks.append(INC(gsem, bi, 16))
                    it += 1
            if my("gpsimd"):
                W(e, "ga", C["ga"])
                W(e, "gb", C["gb"])
            if my("sync") and wdtf_b is not None:
                W(e, "na", C["na"])
                W(e, "nb", C["nb"])
        return ph

    def make_down_phase(x_res, dst, hT_sb, wdT_b, xsl, res_b, ps_d):
        def ph(en, e):
            my = lambda n: en == n
            peg_marks = {}
            dve_marks, st_marks, ldh_marks = [], [], []
            ld2_marks, ldx_marks = {}, {}
            for sc2 in range(4):
                if my("sync") and sc2 > 0:
                    W(e, "pe", peg_marks[(sc2 * 4 - 1, NOC - 1)])
                for ot in range(NOC):
                    bi = (e.dma_start(
                        out=hT_sb[:, ot * 1024:(ot + 1) * 1024],
                        in_=hT_s[ot][:, sc2 * 1024:(sc2 + 1) * 1024])
                        if my("sync") else None)
                    INC("lc", bi, 16)
                ldh_marks.append(C["lc"])
                for hc in range(4):
                    it2 = sc2 * 4 + hc
                    if my("sync") and it2 >= 1:
                        W(e, "dve", dve_marks[it2 - 1])
                    for st8 in range(8):
                        r0 = (sc2 * 8 + st8) * P
                        bi = (e.dma_start(
                            out=xsl[:, st8 * 512:(st8 + 1) * 512],
                            in_=x_res[r0:r0 + P, hc * 512:(hc + 1) * 512])
                            if my("sync") else None)
                        INC("la", bi, 16)
                    ldx_marks[it2] = C["la"]
                    for ot in range(NOC):
                        wpar = ot % 4
                        wsem = ("wa", "wb", "wc", "wd")[wpar]
                        if my("scalar"):
                            if (it2, ot - 4) in peg_marks:
                                W(e, "pe", peg_marks[(it2, ot - 4)])
                            elif ot < 4 and it2 > 0:
                                W(e, "pe", peg_marks[(it2 - 1, NOC - 1)])
                            bi = e.dma_start(
                                out=wdT_b[wpar][:],
                                in_=wdT_nat[ot * P:(ot + 1) * P,
                                            hc * 512:(hc + 1) * 512])
                        else:
                            bi = None
                        ld2_marks[(it2, ot)] = INC(wsem, bi, 16)
                        if my("tensor"):
                            W(e, wsem, ld2_marks[(it2, ot)])
                            if ot == 0:
                                W(e, "lc", ldh_marks[sc2])
                                if it2 >= 1:
                                    W(e, "dve", dve_marks[it2 - 1])
                            for st8 in range(8):
                                bi = e.matmul(
                                    ps_d[:, st8 * 512:(st8 + 1) * 512],
                                    hT_sb[:, ot * 1024 + st8 * P:ot * 1024 + (st8 + 1) * P],
                                    wdT_b[wpar][:],
                                    start=(ot == 0), stop=(ot == NOC - 1))
                        else:
                            bi = None
                        peg_marks[(it2, ot)] = INC("pe", bi)
                    if my("vector"):
                        W(e, "pe", peg_marks[(it2, NOC - 1)])
                        W(e, "la", ldx_marks[it2])
                        if it2 >= 1:
                            W(e, "ga", st_marks[it2 - 1])
                        for st8 in range(8):
                            bi = e.scalar_tensor_tensor(
                                out=res_b[:, st8 * 512:(st8 + 1) * 512],
                                in0=ps_d[:, st8 * 512:(st8 + 1) * 512],
                                scalar=0.5,
                                in1=xsl[:, st8 * 512:(st8 + 1) * 512],
                                op0=ALU.mult, op1=ALU.add)
                    else:
                        bi = None
                    dve_marks.append(INC("dve", bi))
                    if my("gpsimd"):
                        W(e, "dve", dve_marks[it2])
                    for st8 in range(8):
                        r0 = (sc2 * 8 + st8) * P
                        bi = (e.dma_start(
                            out=dst[r0:r0 + P, hc * 512:(hc + 1) * 512],
                            in_=res_b[:, st8 * 512:(st8 + 1) * 512])
                            if my("gpsimd") else None)
                        INC("ga", bi, 16)
                    st_marks.append(C["ga"])
            if my("gpsimd"):
                W(e, "ga", C["ga"])
        return ph

    def emit_ffn(tag, x_src, res_dst):
        es_x = ExitStack()
        xT_sb = es_x.enter_context(nc.sbuf_tensor(f"xT_{tag}", [P, NDT * S], BF16)).ap()

        es_a = ExitStack()
        xin = [es_a.enter_context(nc.sbuf_tensor(f"xin{i}_{tag}", [P, D], F32)).ap()
               for i in range(2)]
        sq_scr = es_a.enter_context(nc.sbuf_tensor(f"sq_{tag}", [P, D], BF16)).ap()
        xln = [es_a.enter_context(nc.sbuf_tensor(f"xln{i}_{tag}", [P, D], BF16)).ap()
               for i in range(2)]
        nsum_t = es_a.enter_context(nc.sbuf_tensor(f"nsum_{tag}", [P, NST], F32)).ap()
        nmean_t = es_a.enter_context(nc.sbuf_tensor(f"nmean_{tag}", [P, NST], F32)).ap()
        rstd_t = es_a.enter_context(nc.sbuf_tensor(f"rstd_{tag}", [P, NST], F32)).ap()
        sc1 = es_a.enter_context(nc.sbuf_tensor(f"sc1_{tag}", [P, 4], F32)).ap()
        sumsq_t = es_a.enter_context(nc.sbuf_tensor(f"ssq_{tag}", [P, NST], F32)).ap()
        var_t = es_a.enter_context(nc.sbuf_tensor(f"var_{tag}", [P, NST], F32)).ap()
        ps_tr = [es_a.enter_context(nc.psum_tensor(f"ps_tr{i}_{tag}", [P, 1024], BF16)).ap()
                 for i in range(2)]
        deq = None
        if tag == "f1":
            idx_b = [es_a.enter_context(nc.sbuf_tensor(f"idx{i}", [P, D], I32)).ap()
                     for i in range(2)]
            idx_bf = es_a.enter_context(nc.sbuf_tensor("idx_bf", [P, D], BF16)).ap()
            tcon = es_a.enter_context(nc.sbuf_tensor("tcon", [P, D], BF16)).ap()
            acc_b = [es_a.enter_context(nc.sbuf_tensor(f"acc{i}", [P, D], BF16)).ap()
                     for i in range(2)]
            deq = (idx_b, idx_bf, tcon, acc_b)
        run_block(f"ln_{tag}", make_ln_phase(
            x_src, xT_sb, (xin, sq_scr, xln, nsum_t, nmean_t, rstd_t, sc1,
                           sumsq_t, var_t), ps_tr, deq=deq))
        es_a.close()

        es_b = ExitStack()
        wT_g = [[es_b.enter_context(nc.sbuf_tensor(f"wTg{i}_{t}_{tag}", [P, P], BF16)).ap()
                 for t in range(NDT)] for i in range(2)]
        wT_u = [[es_b.enter_context(nc.sbuf_tensor(f"wTu{i}_{t}_{tag}", [P, P], BF16)).ap()
                 for t in range(NDT)] for i in range(2)]
        silu_b = [es_b.enter_context(nc.sbuf_tensor(f"si{i}_{tag}", [P, 512], BF16)).ap()
                  for i in range(2)]
        up_b = [es_b.enter_context(nc.sbuf_tensor(f"up{i}_{tag}", [P, 512], BF16)).ap()
                for i in range(2)]
        h_b = [es_b.enter_context(nc.sbuf_tensor(f"hb{i}_{tag}", [P, 512], BF16)).ap()
               for i in range(2)]
        ps_g = [es_b.enter_context(nc.psum_tensor(f"psg{i}_{tag}", [P, 512], F32)).ap()
                for i in range(2)]
        ps_u = [es_b.enter_context(nc.psum_tensor(f"psu{i}_{tag}", [P, 512], F32)).ap()
                for i in range(2)]
        wdtf_b = None
        if tag == "f1":
            wdtf_b = [es_b.enter_context(
                nc.sbuf_tensor(f"wdtf{i}", [P, D], BF16)).ap() for i in range(2)]
        run_block(f"gu_{tag}", make_gateup_phase(
            xT_sb, wT_g, wT_u, silu_b, up_b, h_b, ps_g, ps_u, wdtf_b=wdtf_b))
        es_b.close()
        es_x.close()

        es_c = ExitStack()
        hT_sb = es_c.enter_context(nc.sbuf_tensor(f"hTs_{tag}", [P, NOC * 1024], BF16)).ap()
        wdT_b = [es_c.enter_context(nc.sbuf_tensor(f"wdT{i}_{tag}", [P, 512], BF16)).ap()
                 for i in range(4)]
        xsl = es_c.enter_context(nc.sbuf_tensor(f"xsl_{tag}", [P, 8 * 512], F32)).ap()
        res_b = es_c.enter_context(nc.sbuf_tensor(f"res_{tag}", [P, 8 * 512], F32)).ap()
        ps_d = es_c.enter_context(nc.psum_tensor(f"psd_{tag}", [P, 4096], F32)).ap()
        run_block(f"dn_{tag}", make_down_phase(x_src, res_dst, hT_sb, wdT_b, xsl,
                                               res_b, ps_d))
        es_c.close()

    ffn1_dst = out if upto == "ffn1" else x1_s
    emit_ffn("f1", x_in, ffn1_dst)

    if upto == "ffn1":
        es.close()
        return nc

    es_x = ExitStack()
    xT_sb = es_x.enter_context(nc.sbuf_tensor("xT_at", [P, NDT * S], BF16)).ap()

    es_a = ExitStack()
    xin = [es_a.enter_context(nc.sbuf_tensor(f"xin{i}_at", [P, D], F32)).ap()
           for i in range(2)]
    sq_scr = es_a.enter_context(nc.sbuf_tensor("sq_at", [P, D], BF16)).ap()
    xln = [es_a.enter_context(nc.sbuf_tensor(f"xln{i}_at", [P, D], BF16)).ap()
           for i in range(2)]
    nsum_t = es_a.enter_context(nc.sbuf_tensor("nsum_at", [P, NST], F32)).ap()
    nmean_t = es_a.enter_context(nc.sbuf_tensor("nmean_at", [P, NST], F32)).ap()
    rstd_t = es_a.enter_context(nc.sbuf_tensor("rstd_at", [P, NST], F32)).ap()
    sc1 = es_a.enter_context(nc.sbuf_tensor("sc1_at", [P, 4], F32)).ap()
    sumsq_t = es_a.enter_context(nc.sbuf_tensor("ssq_at", [P, NST], F32)).ap()
    var_t = es_a.enter_context(nc.sbuf_tensor("var_at", [P, NST], F32)).ap()
    ps_tr = [es_a.enter_context(nc.psum_tensor(f"ps_tr{i}_at", [P, 1024], BF16)).ap()
             for i in range(2)]
    run_block("ln_at", make_ln_phase(
        x1_s, xT_sb, (xin, sq_scr, xln, nsum_t, nmean_t, rstd_t, sc1,
                      sumsq_t, var_t), ps_tr))
    es_a.close()

    es_p = ExitStack()
    cod_b = [es_p.enter_context(nc.sbuf_tensor(f"cod{i}", [P, D], F32)).ap()
             for i in range(2)]
    scb_b = [es_p.enter_context(nc.sbuf_tensor(f"scb{i}", [P, D], F32)).ap()
             for i in range(2)]
    w_bf = [es_p.enter_context(nc.sbuf_tensor(f"wbf{i}", [P, D], BF16)).ap()
            for i in range(2)]
    wv_bf = es_p.enter_context(nc.sbuf_tensor("wvbf", [P, NDT * 512], BF16)).ap()
    ev_b = [es_p.enter_context(nc.sbuf_tensor(f"ev{i}", [P, 512], BF16)).ap()
            for i in range(2)]
    ps_p = [es_p.enter_context(nc.psum_tensor(f"psp{i}", [P, 512], F32)).ap()
            for i in range(2)]

    def p3_proj(en, e):
        my = lambda n: en == n
        deq_marks, pe_marks, ev_marks, ld_marks, st_marks = [], [], [], [], []
        pit = 0
        for pj, (cd, sc, bT, dstT) in enumerate((
                (wcod["wq"], wscl["wq"], bqT, qT_sc),
                (wcod["wk"], wscl["wk"], bkT, kT_sc))):
            cd_r = cd.rearrange("(t p) f -> p t f", p=P)
            for fc in range(NDT):
                it3 = pj * NDT + fc
                par = it3 % 2
                lsem = "la" if par == 0 else "lb"
                if my("sync") and it3 >= 2:
                    W(e, "dve", deq_marks[it3 - 2])
                bi = (e.dma_start(out=cod_b[par][:],
                                  in_=cd_r[:, :, fc * P:(fc + 1) * P])
                      if my("sync") else None)
                INC(lsem, bi, 16)
                for t in range(NDT):
                    bi = (e.dma_start(
                        out=scb_b[par][:, t * P:(t + 1) * P],
                        in_=sc[4 * t:4 * t + 4, fc * P:(fc + 1) * P]
                        .unsqueeze(1).broadcast_to([4, 32, P]))
                        if my("sync") else None)
                    INC(lsem, bi, 16)
                ld_marks.append(C[lsem])
                if my("vector"):
                    W(e, lsem, ld_marks[it3])
                    if it3 >= 2:
                        W(e, "pe", pe_marks[(it3 - 2) * 8 + 7])
                    bi = e.tensor_tensor(out=w_bf[par][:], in0=cod_b[par][:],
                                         in1=scb_b[par][:], op=ALU.mult)
                else:
                    bi = None
                deq_marks.append(INC("dve", bi))
                for scn in range(8):
                    ppar = pit % 2
                    ssem = "ga" if ppar == 0 else "gb"
                    if my("tensor"):
                        W(e, "dve", deq_marks[it3])
                        if pit >= 2:
                            W(e, ev_marks[pit - 2][0], ev_marks[pit - 2][1])
                        for t in range(NDT):
                            bi = e.matmul(
                                ps_p[ppar][:],
                                w_bf[par][:, t * P:(t + 1) * P],
                                xT_sb[:, t * S + scn * 512:t * S + (scn + 1) * 512],
                                start=(t == 0), stop=(t == NDT - 1))
                    else:
                        bi = None
                    pe_marks.append(INC("pe", bi))
                    if my("scalar"):
                        W(e, "pe", pe_marks[pit])
                        if pit >= 2:
                            W(e, st_marks[pit - 2][0], st_marks[pit - 2][1])
                        bi = e.activation(out=ev_b[ppar][:], in_=ps_p[ppar][:],
                                          func=AF.Identity, bias=bT[:, fc:fc + 1])
                    else:
                        bi = None
                    ev_marks.append(("act", INC("act", bi)))
                    if my("gpsimd"):
                        W(e, "act", ev_marks[pit][1])
                        bi = e.dma_start(
                            out=dstT[fc * P:(fc + 1) * P,
                                     scn * 512:(scn + 1) * 512],
                            in_=ev_b[ppar][:])
                    else:
                        bi = None
                    st_marks.append((ssem, INC(ssem, bi, 16)))
                    pit += 1
        vdeq_marks = []
        cdv = wcod["wv"]
        scv = wscl["wv"]
        for fg in range(4):
            for t in range(NDT):
                vp = t % 2
                lsem = "la" if vp == 0 else "lb"
                if my("sync"):
                    if len(vdeq_marks) >= 2:
                        W(e, "dve", vdeq_marks[-2])
                    else:
                        W(e, "dve", deq_marks[30 + len(vdeq_marks)])
                bi = (e.dma_start(out=cod_b[vp][:, 0:512],
                                  in_=cdv[t * P:(t + 1) * P,
                                          fg * 512:(fg + 1) * 512])
                      if my("sync") else None)
                INC(lsem, bi, 16)
                bi = (e.dma_start(
                    out=scb_b[vp][:, 0:512],
                    in_=scv[4 * t:4 * t + 4, fg * 512:(fg + 1) * 512]
                    .unsqueeze(1).broadcast_to([4, 32, 512]))
                    if my("sync") else None)
                m_ld = INC(lsem, bi, 16)
                if my("vector"):
                    W(e, lsem, m_ld)
                    if fg > 0 and t == 0:
                        W(e, "pe", pe_marks[-1])
                    bi = e.tensor_tensor(out=wv_bf[:, t * 512:(t + 1) * 512],
                                         in0=cod_b[vp][:, 0:512],
                                         in1=scb_b[vp][:, 0:512], op=ALU.mult)
                else:
                    bi = None
                vdeq_marks.append(INC("dve", bi))
            dq_all = vdeq_marks[-1]
            for stl in range(NST):
                ppar = pit % 2
                ssem = "ga" if ppar == 0 else "gb"
                if my("tensor"):
                    W(e, "dve", dq_all)
                    if pit >= 2:
                        W(e, ev_marks[pit - 2][0], ev_marks[pit - 2][1])
                    for t in range(NDT):
                        bi = e.matmul(
                            ps_p[ppar][:],
                            xT_sb[:, t * S + stl * P:t * S + (stl + 1) * P],
                            wv_bf[:, t * 512:(t + 1) * 512],
                            start=(t == 0), stop=(t == NDT - 1))
                else:
                    bi = None
                pe_marks.append(INC("pe", bi))
                if my("vector"):
                    W(e, "pe", pe_marks[-1])
                    if pit >= 2:
                        W(e, st_marks[pit - 2][0], st_marks[pit - 2][1])
                    bi = e.scalar_tensor_tensor(
                        out=ev_b[ppar][:], in0=ps_p[ppar][:], scalar=1.0,
                        in1=bv_bc[:, fg * 512:(fg + 1) * 512],
                        op0=ALU.mult, op1=ALU.add)
                else:
                    bi = None
                ev_marks.append(("dve", INC("dve", bi)))
                if my("gpsimd"):
                    W(e, "dve", ev_marks[-1][1])
                for j in range(4):
                    bi = (e.dma_start(
                        out=v_sc[4 * fg + j][stl * P:(stl + 1) * P, :],
                        in_=ev_b[ppar][:, j * P:(j + 1) * P])
                        if my("gpsimd") else None)
                    INC(ssem, bi, 16)
                st_marks.append((ssem, C[ssem]))
                pit += 1
        if my("gpsimd"):
            W(e, "ga", C["ga"])
            W(e, "gb", C["gb"])

    run_block("p3_proj", p3_proj)
    es_p.close()
    es_x.close()

    if upto == "qkv":
        es.close()
        return nc

    es_t = ExitStack()
    wo_sb = es_t.enter_context(nc.sbuf_tensor("wo_sb", [P, NDT * D], BF16)).ap()
    aoT = es_t.enter_context(nc.sbuf_tensor("aoT", [P, NH * 512], BF16)).ap()
    kt_b = [es_t.enter_context(nc.sbuf_tensor(f"kt{i}", [P, S], BF16)).ap()
            for i in range(2)]
    v_b = [es_t.enter_context(nc.sbuf_tensor(f"vb{i}", [P, S], BF16)).ap()
           for i in range(2)]
    qt_b = [es_t.enter_context(nc.sbuf_tensor(f"qt{i}", [P, 512], BF16)).ap()
            for i in range(2)]
    exp_b = [es_t.enter_context(nc.sbuf_tensor(f"ex{i}", [P, 1024], BF16)).ap()
             for i in range(4)]
    avbf = [es_t.enter_context(nc.sbuf_tensor(f"avbf{i}", [P, 512], BF16)).ap()
            for i in range(2)]
    rbc = [es_t.enter_context(nc.sbuf_tensor(f"rbc{i}", [P, 512], BF16)).ap()
           for i in range(2)]
    rec_row = [es_t.enter_context(nc.sbuf_tensor(f"rec_row{i}", [1, 512], F32)).ap()
               for i in range(2)]
    codw = es_t.enter_context(nc.sbuf_tensor("codw", [P, D], F32)).ap()
    sclw = es_t.enter_context(nc.sbuf_tensor("sclw", [P, D], F32)).ap()
    x1_b = [es_t.enter_context(nc.sbuf_tensor(f"x1b{i}", [P, D], F32)).ap()
            for i in range(2)]
    xo_b = [es_t.enter_context(nc.sbuf_tensor(f"xob{i}", [P, D], F32)).ap()
            for i in range(2)]
    ps_s = [es_t.enter_context(nc.psum_tensor(f"pss{i}", [P, 1024], F32)).ap()
            for i in range(3)]
    ps_av = [es_t.enter_context(nc.psum_tensor(f"ps_av{i}", [P, 512], F32)).ap()
             for i in range(2)]

    att_dst = out if upto == "attn" else x2_s

    def p3_att(en, e):
        my = lambda n: en == n
        wodeq_marks = []
        cd_r = wcod["wo"].rearrange("(t p) f -> p t f", p=P)
        sc_r = wscl["wo"].rearrange("(t r) f -> r t f", r=4)
        for t in range(NDT):
            if my("sync") and t >= 1:
                W(e, "dve", wodeq_marks[t - 1])
            bi = (e.dma_start(out=codw[:], in_=cd_r[:, t, :])
                  if my("sync") else None)
            INC("lc", bi, 16)
            bi = (e.dma_start(out=sclw[:],
                              in_=sc_r[:, t, :].unsqueeze(1)
                              .broadcast_to([4, 32, D]))
                  if my("sync") else None)
            m_ld = INC("lc", bi, 16)
            if my("vector"):
                W(e, "lc", m_ld)
                bi = e.tensor_tensor(out=wo_sb[:, t * D:(t + 1) * D],
                                     in0=codw[:], in1=sclw[:], op=ALU.mult)
            else:
                bi = None
            wodeq_marks.append(INC("dve", bi))
        wo_ready = wodeq_marks[-1]

        sc_marks, ex_marks, av_marks = {}, {}, {}
        ao_marks, rec_marks, rbc_marks, avc_marks, ld3_marks = {}, {}, {}, {}, {}
        woe_marks, wost_marks, wo_end, ldx_marks = {}, {}, {}, {}

        for qsc in range(8):
            for h in range(NH):
                ih = qsc * NH + h
                par = ih % 2
                bA = ps_av[ih % 2]
                bD = ps_av[(ih + 1) % 2]
                lsem = "la" if par == 0 else "lb"
                if my("sync") and ih >= 2:
                    W(e, "pe", av_marks[(ih - 2) * 16 + 15])
                bi = (e.dma_start(out=kt_b[par][:],
                                  in_=kT_sc[h * P:(h + 1) * P, :])
                      if my("sync") else None)
                INC(lsem, bi, 16)
                bi = (e.dma_start(out=v_b[par][:],
                                  in_=v_sc[h].rearrange("(kt p) d -> p kt d", p=P))
                      if my("sync") else None)
                INC(lsem, bi, 16)
                bi = (e.dma_start(out=qt_b[par][:],
                                  in_=qT_sc[h * P:(h + 1) * P,
                                            qsc * 512:(qsc + 1) * 512])
                      if my("sync") else None)
                ld3_marks[ih] = INC(lsem, bi, 16)
                for gg in range(18):
                    if gg < 16:
                        grp = gg
                        gctr = ih * 16 + grp
                        spar = gctr % 3
                        epar = gctr % 4
                        if my("tensor"):
                            if grp == 0:
                                W(e, lsem, ld3_marks[ih])
                                if h == 0 and qsc >= 1:
                                    W(e, "pe", wo_end[qsc - 1])
                                    W(e, "dve", woe_marks[(qsc - 1, 3, 3)])
                            if gctr >= 3:
                                W(e, "act", ex_marks[gctr - 3])
                            for i in range(2):
                                kt = 2 * grp + i
                                bi = e.matmul(ps_s[spar][:, i * 512:(i + 1) * 512],
                                              kt_b[par][:, kt * P:(kt + 1) * P],
                                              qt_b[par][:], start=True, stop=True)
                        else:
                            bi = None
                        sc_marks[gctr] = INC("pe", bi)
                        if my("scalar"):
                            W(e, "pe", sc_marks[gctr])
                            if gctr >= 4:
                                W(e, "pe", av_marks[gctr - 4])
                            bi = e.activation(out=exp_b[epar][:], in_=ps_s[spar][:],
                                              func=AF.Exp, bias=mbias[:, 0:1],
                                              scale=INV_SQRT_HD)
                        else:
                            bi = None
                        ex_marks[gctr] = INC("act", bi)
                    if gg >= 2:
                        grp = gg - 2
                        gctr = ih * 16 + grp
                        epar = gctr % 4
                        if my("tensor"):
                            W(e, "act", ex_marks[gctr])
                            if grp == 0 and ih >= 1:
                                W(e, "dve", rec_marks[ih - 1])
                                W(e, "dve", avc_marks[ih - 1])
                            for i in range(2):
                                kt = 2 * grp + i
                                e.matmul(bA[:],
                                         v_b[par][:, kt * P:(kt + 1) * P],
                                         exp_b[epar][:, i * 512:(i + 1) * 512],
                                         start=(kt == 0), stop=(kt == 31))
                                bi = e.matmul(bD[0:1, :], ones_col[:],
                                              exp_b[epar][:, i * 512:(i + 1) * 512],
                                              start=(kt == 0), stop=(kt == 31))
                        else:
                            bi = None
                        av_marks[gctr] = INC("pe", bi)
                if my("vector"):
                    W(e, "pe", av_marks[ih * 16 + 15])
                    if ih >= 2:
                        W(e, "gc", rbc_marks[ih - 2])
                    bi = e.reciprocal(out=rec_row[ih % 2][:], in_=bD[0:1, :])
                else:
                    bi = None
                rec_marks[ih] = INC("dve", bi)
                tpar = ih % 2
                if my("gpsimd"):
                    W(e, "dve", rec_marks[ih])
                    if ih >= 2:
                        W(e, "dve", ao_marks[ih - 2])
                    bi = e.dma_start(out=rec_dr[tpar], in_=rec_row[tpar][:])
                else:
                    bi = None
                m_gc = INC("gc", bi, 16)
                if my("gpsimd"):
                    W(e, "gc", m_gc)
                    bi = e.dma_start(out=rbc[tpar][:],
                                     in_=rec_dr[tpar].broadcast_to([P, 512]))
                else:
                    bi = None
                rbc_marks[ih] = INC("gc", bi, 16)
                if my("vector"):
                    if h == 0 and qsc >= 1:
                        W(e, "pe", wo_end[qsc - 1])
                    if ih >= 2:
                        W(e, "dve", ao_marks[ih - 2])
                    bi = e.tensor_copy(out=avbf[tpar][:], in_=bA[:])
                else:
                    bi = None
                avc_marks[ih] = INC("dve", bi)
                if my("vector"):
                    W(e, "dve", avc_marks[ih])
                    W(e, "gc", rbc_marks[ih])
                    bi = e.tensor_tensor(out=aoT[:, h * 512:(h + 1) * 512],
                                         in0=avbf[tpar][:], in1=rbc[tpar][:],
                                         op=ALU.mult)
                else:
                    bi = None
                ao_marks[ih] = INC("dve", bi)
            for stl in range(4):
                spar2 = stl % 2
                wsem = "wa" if (qsc * 4 + stl) % 2 == 0 else "wb"
                r0 = (qsc * 4 + stl) * P
                if my("sync"):
                    prev = qsc * 4 + stl - 2
                    if prev >= 0:
                        W(e, "dve", woe_marks[(prev // 4, prev % 4, 3)])
                    bi = e.dma_start(out=x1_b[spar2][:], in_=x1_s[r0:r0 + P, :])
                else:
                    bi = None
                ldx_marks[(qsc, stl)] = INC(wsem, bi, 16)
                for dmn in range(4):
                    widx = stl * 4 + dmn
                    wps = ps_s[widx % 2][:, 0:512]
                    if my("tensor"):
                        if stl == 0 and dmn == 0:
                            W(e, "dve", ao_marks[qsc * NH + NH - 1])
                            W(e, "dve", wo_ready)
                            W(e, "act", ex_marks[qsc * NH * 16 + NH * 16 - 1])
                        prev = (qsc * 16 + widx) - 2
                        if prev >= 0:
                            pq, prem = divmod(prev, 16)
                            W(e, "dve", woe_marks[(pq, prem // 4, prem % 4)])
                        for t in range(NDT):
                            bi = e.matmul(
                                wps,
                                aoT[:, t * 512 + stl * P:t * 512 + (stl + 1) * P],
                                wo_sb[:, t * D + dmn * 512:t * D + (dmn + 1) * 512],
                                start=(t == 0), stop=(t == NDT - 1))
                    else:
                        bi = None
                    m_pe = INC("pe", bi)
                    if dmn == 3:
                        wo_end[qsc] = m_pe
                    if my("vector"):
                        W(e, "pe", m_pe)
                        W(e, wsem, ldx_marks[(qsc, stl)])
                        prev2 = qsc * 4 + stl - 2
                        if prev2 >= 0:
                            W(e, wost_marks[(prev2 // 4, prev2 % 4)][0],
                              wost_marks[(prev2 // 4, prev2 % 4)][1])
                        bi = e.scalar_tensor_tensor(
                            out=xo_b[spar2][:, dmn * 512:(dmn + 1) * 512],
                            in0=wps, scalar=1.0,
                            in1=bo_bc[:, dmn * 512:(dmn + 1) * 512],
                            op0=ALU.mult, op1=ALU.add)
                    else:
                        bi = None
                    EDGE(my("vector"), e, "dve", bi)
                    bi = (e.tensor_tensor(
                        out=xo_b[spar2][:, dmn * 512:(dmn + 1) * 512],
                        in0=xo_b[spar2][:, dmn * 512:(dmn + 1) * 512],
                        in1=x1_b[spar2][:, dmn * 512:(dmn + 1) * 512],
                        op=ALU.add) if my("vector") else None)
                    woe_marks[(qsc, stl, dmn)] = INC("dve", bi)
                ssem = "ga" if spar2 == 0 else "gb"
                if my("gpsimd"):
                    W(e, "dve", woe_marks[(qsc, stl, 3)])
                    bi = e.dma_start(out=att_dst[r0:r0 + P, :], in_=xo_b[spar2][:])
                else:
                    bi = None
                wost_marks[(qsc, stl)] = (ssem, INC(ssem, bi, 16))
        if my("gpsimd"):
            W(e, "ga", C["ga"])
            W(e, "gb", C["gb"])

    run_block("p3_att", p3_att)
    es_t.close()

    if upto == "attn":
        es.close()
        return nc

    emit_ffn("f2", x2_s, out)

    es.close()
    return nc


def kernel(**inputs):
    x = np.asarray(inputs["L0_attn_x"])
    lut = np.asarray(inputs["L0_ffn_lut"]).reshape(1, 16)
    gate_idx = np.asarray(inputs["L0_ffn_gate_idx"])
    up_idx = np.asarray(inputs["L0_ffn_up_idx"])
    down_idx = np.asarray(inputs["L0_ffn_down_idx"])
    gate_pad = np.vstack([gate_idx, gate_idx[:6144 - IFF]])
    up_pad = np.vstack([up_idx, up_idx[:6144 - IFF]])
    common = {
        "lut": np.ascontiguousarray(lut),
        "wq_codes": np.ascontiguousarray(inputs["L0_attn_wq_codes"]),
        "wk_codes": np.ascontiguousarray(inputs["L0_attn_wk_codes"]),
        "wv_codes": np.ascontiguousarray(inputs["L0_attn_wv_codes"]),
        "wo_codes": np.ascontiguousarray(inputs["L0_attn_wo_codes"]),
        "wq_scales": np.ascontiguousarray(inputs["L0_attn_wq_scales"]),
        "wk_scales": np.ascontiguousarray(inputs["L0_attn_wk_scales"]),
        "wv_scales": np.ascontiguousarray(inputs["L0_attn_wv_scales"]),
        "wo_scales": np.ascontiguousarray(inputs["L0_attn_wo_scales"]),
        "bq": np.ascontiguousarray(np.asarray(inputs["L0_attn_bq"]).reshape(1, D)),
        "bk": np.ascontiguousarray(np.asarray(inputs["L0_attn_bk"]).reshape(1, D)),
        "bv": np.ascontiguousarray(np.asarray(inputs["L0_attn_bv"]).reshape(1, D)),
        "bo": np.ascontiguousarray(np.asarray(inputs["L0_attn_bo"]).reshape(1, D)),
    }
    nc = build("full")
    in_maps = [dict(common,
                    x=np.ascontiguousarray(x[b]),
                    gate_sh=np.ascontiguousarray(gate_pad[768 * b:768 * (b + 1)]),
                    up_sh=np.ascontiguousarray(up_pad[768 * b:768 * (b + 1)]),
                    down_sh=np.ascontiguousarray(down_idx[256 * b:256 * (b + 1)]))
               for b in range(8)]
    res = run_bass_kernel_spmd(nc, in_maps, core_ids=list(range(8)))
    return np.stack([np.asarray(r["out"], dtype=np.float32)
                     for r in res.results], axis=0)
